# version 01
# baseline (speedup 1.0000x reference)
import math
from contextlib import ExitStack

import numpy as np
import ml_dtypes

import concourse.bass as bass
import concourse.mybir as mybir
from concourse.bass_utils import run_bass_kernel_spmd

F32 = mybir.dt.float32
BF16 = mybir.dt.bfloat16
AF = mybir.ActivationFunctionType
ALU = mybir.AluOpType
AX = mybir.AxisListType
BF = ml_dtypes.bfloat16

D, S, NT, ND = 1024, 2048, 16, 8
H, HD, NPAIR = 16, 64, 8
E, HH, NHT = 8, 2048, 16
KBy = 1024
EXPS = 0.125
ENGS = ("tensor", "vector", "scalar", "gpsimd", "sync")
SL = 128 * ND * 128


def A(t, off, dims):
    if isinstance(t, bass.AP):
        t = t.tensor
    return bass.AP(t, off, [list(d) for d in dims])


def host_constants():
    c = {}
    dl = 2048 - 16 + 1
    nums = np.linspace(0.0, 1.0, dl, dtype=np.float64)
    dvv = 1.0 - np.power(nums, 1.0 / np.e)
    dvv = np.concatenate([np.ones(15), dvv])[:2048]
    kk, mm = np.meshgrid(np.arange(128), np.arange(2560), indexing="ij")
    delta = (mm - 512) - kk
    c["DSTRIP"] = np.where((delta >= 0) & (delta < 2048),
                           dvv[np.clip(delta, 0, 2047)], 0.0).astype(BF)
    kk, mm = np.meshgrid(np.arange(128), np.arange(1024), indexing="ij")
    c["CSTR"] = (((mm - 512) - kk) >= 0).astype(BF)
    kk, mm = np.meshgrid(np.arange(128), np.arange(1024), indexing="ij")
    d2 = (mm + 128 - 512) - kk
    c["WSTR"] = ((d2 >= 0) & (d2 <= 128)).astype(BF)
    inv = 1.0 / 10000.0 ** (np.arange(32, dtype=np.float64) / 32.0)
    ang = np.outer(inv, np.arange(S, dtype=np.float64))
    c["COSF"] = np.tile(np.cos(ang), (4, 1)).astype(BF)
    c["SINF"] = np.tile(np.sin(ang), (4, 1)).astype(BF)
    c["RSGN"] = np.tile(np.r_[-np.ones(32), np.ones(32)], 2).astype(np.float32).reshape(128, 1)
    swap = np.arange(128)
    swap = (swap // 64) * 64 + ((swap + 32) % 64)
    perm = np.zeros((128, 128), dtype=BF)
    perm[swap, np.arange(128)] = 1.0
    c["PERM"] = perm
    c["ONES1"] = np.ones((1, 128), dtype=BF)
    c["IDENTF"] = np.eye(128, dtype=np.float32)
    c["ONES1F"] = np.ones((1, 128), dtype=np.float32)
    c["ONESC"] = np.ones((128, 1), dtype=BF)
    return c


def _pp(b):
    n = b.shape[0] // 128
    return np.ascontiguousarray(b.reshape(n, 128).T)


class Build:
    def __init__(self, upto=4):
        self.upto = upto
        self.nc = bass.Bass("TRN2", target_bir_lowering=False)
        self.q = {k: [] for k in ENGS}
        self.c = {}
        self.es = ExitStack()
        self._mk_io()
        self._mk_sbuf()
        self.psS = self.nc.alloc_psum_tensor("psS", [128, 2048], F32)
        self.psT = self.nc.alloc_psum_tensor("psT", [128, 2048], F32)
        self.SM = {}
        for name in ("qw", "qg", "qo", "pe", "dv", "sc", "gp", "sy"):
            self.SM[name] = self.es.enter_context(self.nc.semaphore(name))
            self.c[name] = 0

    def em(self, eng, fn):
        self.q[eng].append(fn)

    def bump(self, sem, k=1):
        self.c[sem] += k
        return self.c[sem]

    def cur(self, sem):
        return self.c[sem]

    def wait(self, eng, sem, val):
        if val > 0:
            self.em(eng, lambda e, s=self.SM[sem], v=val: e.wait_ge(s, v))

    def barrier(self):
        vals = {k: self.cur(k) for k in self.SM}
        own = {"tensor": "pe", "vector": "dv", "scalar": "sc", "gpsimd": "gp", "sync": "sy"}
        for eng in ENGS:
            for sem, v in vals.items():
                if v > 0 and sem != own.get(eng):
                    self.wait(eng, sem, v)

    def xfence(self, eng):
        sem = {"tensor": "pe", "vector": "dv", "scalar": "sc", "gpsimd": "gp"}[eng]
        if eng == "scalar":
            v1 = self.bump(sem)
            self.em(eng, lambda e, s=self.SM[sem]: e.drain().then_inc(s, 1))
            return v1
        v1 = self.bump(sem)
        self.em(eng, lambda e, s=self.SM[sem]: e.engine_nop().then_inc(s, 1)
                if hasattr(e, "engine_nop") else e.nop().then_inc(s, 1))
        self.wait(eng, sem, v1)
        v2 = self.bump(sem)
        self.em(eng, lambda e, s=self.SM[sem]: e.nop().then_inc(s, 1))
        return v2

    def scdrain(self):
        v = self.bump("sc")
        self.em("scalar", lambda e, s=self.SM["sc"]: e.drain().then_inc(s, 1))
        return v

    def dma(self, queue, dst, src, transpose=False):
        eng = "sync" if queue == "qw" else "gpsimd"
        v = self.bump(queue, 16)
        s = self.SM[queue]
        if transpose:
            self.em(eng, lambda e, d=dst, q=src: e.dma_start_transpose(d, q).then_inc(s, 16))
        else:
            self.em(eng, lambda e, d=dst, q=src: e.dma_start(d, q).then_inc(s, 16))
        return v

    def _mk_io(self):
        nc = self.nc
        dt = nc.dram_tensor
        self.xin = dt("xin", [2, S, D], F32, kind="ExternalInput")
        self.xout = dt("xout", [2, S, D], F32, kind="ExternalOutput")
        w = {}
        w["aW"] = dt("aW", [D, 3 * D], F32, kind="ExternalInput")
        w["aP"] = dt("aP", [D, D], F32, kind="ExternalInput")
        for nm in ("bWq", "bWk", "bWv", "bWo"):
            w[nm] = dt(nm, [D, D], F32, kind="ExternalInput")
        w["fqkv"] = dt("fqkv", [3 * D, D], F32, kind="ExternalInput")
        w["fout"] = dt("fout", [D, D], F32, kind="ExternalInput")
        w["fgate"] = dt("fgate", [E, D], F32, kind="ExternalInput")
        w["fw1"] = dt("fw1", [E, HH, D], F32, kind="ExternalInput")
        w["fw2"] = dt("fw2", [E, HH, D], F32, kind="ExternalInput")
        w["fw3"] = dt("fw3", [E, D, HH], F32, kind="ExternalInput")
        self.w = w
        v = {}
        for nm, sh, ddt in [("bq_pp", [128, 8], F32), ("bk_pp", [128, 8], F32),
                            ("bv_pp", [128, 8], F32), ("fbq_pp", [128, 8], F32),
                            ("fbk_pp", [128, 8], F32), ("fbv_pp", [128, 8], F32),
                            ("b1_pp", [128, E * NHT], F32), ("b2_pp", [128, E * NHT], F32),
                            ("bo_row", [1, D], F32), ("fob_row", [1, D], F32),
                            ("b3_r", [E, D], F32),
                            ("g1_rep", [128, D], F32), ("g2_rep", [128, D], F32)]:
            v[nm] = dt(nm, sh, ddt, kind="ExternalInput")
        self.vins = v
        cshapes = dict(DSTRIP=[128, 2560], CSTR=[128, 1024], WSTR=[128, 1024],
                       COSF=[128, S], SINF=[128, S], PERM=[128, 128],
                       ONES1=[1, 128], ONESC=[128, 1])
        self.cin = {k: dt("c_" + k, sh, BF16, kind="ExternalInput") for k, sh in cshapes.items()}
        self.cin["ONES1F"] = dt("c_ONES1F", [1, 128], F32, kind="ExternalInput")
        self.cin["IDENTF"] = dt("c_IDENTF", [128, 128], F32, kind="ExternalInput")
        self.cin["RSGN"] = dt("c_RSGN", [128, 1], F32, kind="ExternalInput")
        self.sc_slab = {
            "fq": dt("sl_fq", [8 * SL], BF16), "fk": dt("sl_fk", [8 * SL], BF16),
            "bq": dt("sl_bq", [8 * SL], BF16), "bk": dt("sl_bk", [8 * SL], BF16),
            "aq": dt("sl_aq", [8 * SL], BF16), "ak": dt("sl_ak", [8 * SL], BF16),
            "w1": dt("sl_w1", [E * NHT * SL], BF16), "w2": dt("sl_w2", [E * NHT * SL], BF16),
            "w3": dt("sl_w3", [E * ND * 128 * NHT * 128], BF16),
        }
        for nm in ("fv", "bv", "av"):
            self.sc_slab[nm] = dt("sl_" + nm, [8 * SL], BF16)
        self.sc_row = {
            "fo": dt("ro_fo", [D, D], BF16),
            "bo": dt("ro_bo", [D, D], BF16), "ao": dt("ro_ao", [D, D], BF16),
            "gate": dt("ro_gate", [D, 16], BF16),
        }
        self.xspill = dt("xspill", [S, D], F32)

    def _mk_sbuf(self):
        BASE = (self.nc.sbuf_base + 255) // 256 * 256
        assert BASE <= 17 * KBy, f"unexpected sbuf_base {self.nc.sbuf_base}"
        self._sbuf_BASE = BASE
        _at0 = self.nc.alloc_sbuf_tensor_at

        def at(name, shape, dtype, offset=0):
            return _at0(name, shape, dtype, offset=offset + BASE)
        self.XR = at("XR", [128, NT, D], F32, offset=0)
        self.V_A = at("V_A", [128, NT, D], BF16, offset=0)
        self.OT_A = at("OT_A", [128, NPAIR, S], BF16, offset=32 * KBy)
        self.PSTG = at("PSTG", [128, 8, 3072], BF16, offset=0)
        self.TR = at("TR", [128, NT, D], F32, offset=64 * KBy)
        self.HT = at("HT", [128, NHT, S], BF16, offset=64 * KBy)
        self.V_B = at("V_B", [128, NT, D], BF16, offset=64 * KBy)
        self.OT_B = at("OT_B", [128, NPAIR, S], BF16, offset=96 * KBy)
        self.TSTG = at("TSTG", [128, 192, 128], BF16, offset=64 * KBy)
        self.XNT = at("XNT", [128, ND, S], BF16, offset=128 * KBy)
        o = [160 * KBy]

        def nx(name, shape, dtype, nbytes, share=None):
            t = at(name, shape, dtype, offset=o[0] if share is None else share)
            if share is None:
                self._loff = o[0]
                o[0] += nbytes
            return t

        self.PB = nx("PB", [128, 3, KBy], BF16, 6 * KBy)
        po = self._loff
        self.XTOK = nx("XTOK", [128, 2, D], BF16, 0, share=po)
        self.B3ROW = nx("B3ROW", [1, D], BF16, 0, share=po)
        self.CMB = nx("CMB", [128, S], BF16, 0, share=po + 2 * KBy)
        self.QT = nx("QT", [128, S], BF16, 4 * KBy)
        qo = self._loff
        self.KT = nx("KT", [128, S], BF16, 4 * KBy)
        self.OPW = nx("OPW", [128, 8, 1024], BF16, 0, share=qo)
        self.YBUF = nx("YBUF", [128, S], BF16, 0, share=qo)
        self.YTOK = nx("YTOK", [128, NT, 128], BF16, 0, share=qo + 4 * KBy)
        self.RR = nx("RR", [1, KBy], F32, 4 * KBy)
        self.S1 = nx("S1", [128, KBy], BF16, 0, share=self._loff)
        self.CROW = nx("CROW", [1, S], BF16, 0, share=self._loff)
        self.WB1 = nx("WB1", [128, 2, ND, 128], BF16, 4 * KBy)
        self.WB2 = nx("WB2", [128, 2, ND, 128], BF16, 4 * KBy)
        self.ST6 = nx("ST6", [128, NT, 12], F32, 768)
        st6o = self._loff
        self.STM = nx("STM", [128, 2, NT], F32, 128)
        self.GA = nx("GA", [128, NT, E], F32, 512)
        go = self._loff
        self.GB = nx("GB", [128, NT, E], F32, 512)
        self.GC = nx("GC", [128, NT, E], F32, 512)
        self.GSM = nx("GSM", [128, 4, NT], F32, 256)
        self.GATET = nx("GATET", [128, ND, 16], BF16, 256)
        self.RLB = nx("RLB", [128, 512], BF16, 0, share=go)
        self.BQ = nx("BQ", [128, 8], F32, 32)
        self.BK = nx("BK", [128, 8], F32, 32)
        self.BV = nx("BV", [128, 8], F32, 32)
        self.FBQ = nx("FBQ", [128, 8], F32, 32)
        self.FBK = nx("FBK", [128, 8], F32, 32)
        self.FBV = nx("FBV", [128, 8], F32, 32)
        self.B1B = nx("B1B", [128, E * NHT], F32, 512)
        self.B2B = nx("B2B", [128, E * NHT], F32, 512)
        self.G1 = nx("G1", [128, D], BF16, 2 * KBy)
        self.G2 = nx("G2", [128, D], BF16, 2 * KBy)
        self.ARENA = nx("ARENA", [128, 6144], BF16, 12 * KBy)
        ao_ = self._loff
        self.WB3 = nx("WB3", [128, 2, NHT, 128], BF16, 0, share=ao_)
        self.CO_IDENTF = nx("CO_IDENTF", [128, 128], F32, 0, share=st6o)
        self.CO_PERM = nx("CO_PERM", [128, 128], BF16, 256)
        self.CO_ONES1 = nx("CO_ONES1", [1, 128], BF16, 256)
        self.CO_ONES1F = nx("CO_ONES1F", [1, 128], F32, 512)
        self.CO_ONESC = nx("CO_ONESC", [128, 1], BF16, 32)
        self.CO_RSGN = nx("CO_RSGN", [128, 1], F32, 64)
        self.CPOOL = nx("CPOOL", [128, 8], F32, 32)
        assert o[0] + self._sbuf_BASE - 16 * KBy <= int(207.75 * KBy), f"sbuf over: {o[0]/1024:.1f}K"
        self.sbuf_top = o[0]

    def finalize(self):
        with self.nc.Block() as block:
            for eng in ENGS:
                getattr(block, eng)(self._body(self.q[eng]))
        self.es.close()
        return self.nc

    @staticmethod
    def _body(fns):
        def body(engine):
            for f in fns:
                f(engine)
        return body

    def prep_const_loads(self):
        cv = self.cin
        vv = self.vins
        pairs = [
            (self.CO_PERM[:, :], cv["PERM"]), (self.CO_ONES1[:, :], cv["ONES1"]),
            (self.CO_ONES1F[:, :], cv["ONES1F"]), (self.CO_ONESC[:, :], cv["ONESC"]),
            (self.CO_RSGN[:, :], cv["RSGN"]), (self.CO_IDENTF[:, :], cv["IDENTF"]),
            (self.BQ[:, :], vv["bq_pp"]), (self.BK[:, :], vv["bk_pp"]),
            (self.BV[:, :], vv["bv_pp"]),
            (self.FBQ[:, :], vv["fbq_pp"]), (self.FBK[:, :], vv["fbk_pp"]),
            (self.FBV[:, :], vv["fbv_pp"]),
            (self.B1B[:, :], vv["b1_pp"]), (self.B2B[:, :], vv["b2_pp"]),
            (self.G1[:, :], vv["g1_rep"]), (self.G2[:, :], vv["g2_rep"]),
        ]
        for d, s in pairs:
            self.dma("qg", d, s.ap())
        return self.cur("qg")

    def load_arena(self, kind):
        AR = self.ARENA
        W = 6144
        if kind == "a":
            self.dma("qg", A(AR, 0, [[W, 128], [1, 2560]]), self.cin["DSTRIP"].ap())
            v = self.dma("qg", A(AR, 2560, [[W, 128], [1, 1024]]), self.cin["CSTR"].ap())
        else:
            self.dma("qg", A(AR, 0, [[W, 128], [1, 2048]]), self.cin["COSF"].ap())
            v = self.dma("qg", A(AR, 2048, [[W, 128], [1, 2048]]), self.cin["SINF"].ap())
            row = self.vins["fob_row"] if kind == "inner" else self.vins["bo_row"]
            if kind == "b":
                self.dma("qg", A(AR, 4096, [[W, 128], [1, 1024]]), self.cin["WSTR"].ap())
            orow_c = 4096 if kind == "inner" else 5120
            v = self.dma("qg", A(AR, orow_c, [[W, 128], [1, D]])[0:1, :], row.ap())
        return v

    def prep_weights(self):
        def prep_OD(src_off, srct, nch, ndt, slab=None, row=None):
            Dp = ndt * 128
            stage = A(self.PSTG, 0, [[8 * 3072, 128], [Dp, nch], [1, Dp]])
            vl = self.dma("qg", stage, A(srct, src_off, [[Dp, 128], [128 * Dp, nch], [1, Dp]]))
            self.wait("sync", "qg", vl)
            tin = A(self.PSTG, 0, [[8 * 3072, 128], [128, nch * ndt], [1, 128]])
            tout = A(self.TSTG, 0, [[192 * 128, 128], [128, nch * ndt], [1, 128]])
            self.dma("qw", tout, tin, transpose=True)
            last = 0
            if slab is not None:
                dram, base = slab
                outap = A(dram, base, [[ndt * 128, 128], [128 * ndt * 128, nch],
                                       [1, ndt * 128]])
                inap = A(self.TSTG, 0, [[192 * 128, 128], [ndt * 128, nch],
                                        [1, ndt * 128]])
                last = self.dma("qw", outap, inap)
            if row is not None:
                dram, ncols, col0 = row
                for dtt in range(ndt):
                    outap = A(dram, col0 + dtt * 128 * ncols,
                              [[ncols, 128], [128, nch], [1, 128]])
                    inap = A(self.TSTG, dtt * 128,
                             [[192 * 128, 128], [ndt * 128, nch], [1, 128]])
                    last = self.dma("qw", outap, inap)
            self.wait("gpsimd", "qw", last)
            return last

        sl, ro, w = self.sc_slab, self.sc_row, self.w
        self._prep_ao()
        prep_OD(0, w["fqkv"], 8, ND, slab=(sl["fq"], 0))
        prep_OD(8 * 128 * D, w["fqkv"], 8, ND, slab=(sl["fk"], 0))
        prep_OD(16 * 128 * D, w["fqkv"], 8, ND, slab=(sl["fv"], 0))
        prep_OD(0, w["fout"], 8, ND, row=(ro["fo"], D, 0))
        if self.upto >= 3:
            prep_OD(0, w["bWq"], 8, ND, slab=(sl["bq"], 0))
            prep_OD(0, w["bWk"], 8, ND, slab=(sl["bk"], 0))
            prep_OD(0, w["bWv"], 8, ND, slab=(sl["bv"], 0))
            prep_OD(0, w["bWo"], 8, ND, row=(ro["bo"], D, 0))
        vz = self.bump("dv")
        self.em("vector", lambda e: e.memset(
            A(self.PSTG, 0, [[8 * 3072, 16], [1, D]]), 0.0).then_inc(self.SM["dv"], 1))
        self.wait("gpsimd", "dv", vz)
        vg = self.dma("qg", A(self.PSTG, 0, [[8 * 3072, 8], [1, D]]),
                      A(self.w["fgate"], 0, [[D, 8], [1, D]]))
        self.wait("sync", "qg", vg)
        tin = A(self.PSTG, 0, [[8 * 3072, 16], [128, ND], [1, 128]])
        tout = A(self.TSTG, 0, [[192 * 128, 128], [16, ND], [1, 16]])
        self.dma("qw", tout, tin, transpose=True)
        vgs = self.dma("qw", A(ro["gate"], 0, [[16, 128], [128 * 16, ND], [1, 16]]),
                       A(self.TSTG, 0, [[192 * 128, 128], [16, ND], [1, 16]]))
        self.wait("gpsimd", "qw", vgs)
        if self.upto >= 2:
            for e in range(E):
                prep_OD(e * HH * D, w["fw1"], 16, ND, slab=(sl["w1"], e * NHT * SL))
                prep_OD(e * HH * D, w["fw2"], 16, ND, slab=(sl["w2"], e * NHT * SL))
                prep_OD(e * D * HH, w["fw3"], 8, NHT, slab=(sl["w3"], e * ND * 128 * NHT * 128))
        self.prep_done = (self.cur("qw"), self.cur("qg"))

    def _prep_ao(self):
        vl = self.dma("qg", self.PSTG[:, :, :],
                      A(self.w["aW"], 0, [[3072, 128], [128 * 3072, 8], [1, 3072]]))
        self.wait("sync", "qg", vl)
        va = 0
        for name, ocb in (("aq", 0), ("ak", 8), ("av", 16)):
            for oc in range(8):
                outap = A(self.sc_slab[name], oc * SL,
                          [[ND * 128, 128], [128, ND], [1, 128]])
                inap = A(self.PSTG, (ocb + oc) * 128,
                         [[8 * 3072, 128], [3072, ND], [1, 128]])
                va = self.dma("qw", outap, inap)
        self.wait("gpsimd", "qw", va)
        vp = self.dma("qg", A(self.PSTG, 0, [[8 * 3072, 128], [D, ND], [1, D]]),
                      A(self.w["aP"], 0, [[D, 128], [128 * D, ND], [1, D]]))
        self.wait("sync", "qg", vp)
        vs = self.dma("qw", A(self.sc_row["ao"], 0, [[D, 128], [128 * D, ND], [1, D]]),
                      A(self.PSTG, 0, [[8 * 3072, 128], [D, ND], [1, D]]))
        self.wait("gpsimd", "qw", vs)

    def load_x(self, b):
        return self.dma("qg", self.XR[:, :, :],
                        A(self.xin, b * S * D, [[D, 128], [128 * D, NT], [1, D]]))

    def store_x(self, b):
        return self.dma("qo", A(self.xout, b * S * D, [[D, 128], [128 * D, NT], [1, D]]),
                        self.XR[:, :, :])

    def spill_x(self):
        return self.dma("qo", A(self.xspill, 0, [[D, 128], [128 * D, NT], [1, D]]),
                        self.XR[:, :, :])

    def reload_x(self):
        return self.dma("qg", self.XR[:, :, :],
                        A(self.xspill, 0, [[D, 128], [128 * D, NT], [1, D]]))

    def norm_pass(self, src, mode, dep=None):
        if dep is not None:
            self.wait("vector", dep[0], dep[1])
            self.wait("scalar", dep[0], dep[1])
        for tt in range(NT):
            self.em("vector", lambda e, t=tt: e.bn_stats(self.ST6[:, t, 0:6],
                                                         src[:, t, 0:512]))
            self.em("vector", lambda e, t=tt: e.bn_stats(self.ST6[:, t, 6:12],
                                                         src[:, t, 512:1024]))
        self.xfence("vector")
        for tt in range(NT):
            self.em("vector", lambda e, t=tt: e.bn_aggr(
                self.GA[:, t, 0:2], A(self.ST6, t * 12, [[NT * 12, 128], [1, 6]])))
            self.em("vector", lambda e, t=tt: e.bn_aggr(
                self.GA[:, t, 2:4], A(self.ST6, t * 12 + 6, [[NT * 12, 128], [1, 6]])))
        self.xfence("vector")
        m1v = A(self.GA, 0, [[NT * E, 128], [E, NT]])
        v1v = A(self.GA, 1, [[NT * E, 128], [E, NT]])
        m2v = A(self.GA, 2, [[NT * E, 128], [E, NT]])
        v2v = A(self.GA, 3, [[NT * E, 128], [E, NT]])
        tmpd = A(self.GA, 4, [[NT * E, 128], [E, NT]])
        smean = self.STM[:, 0, :]
        svar = self.STM[:, 1, :]

        def VV(f):
            self.em("vector", f)
            self.xfence("vector")

        VV(lambda e: e.tensor_add(smean, m1v, m2v))
        VV(lambda e: e.tensor_scalar(smean, smean, 0.5, None, ALU.mult))
        VV(lambda e: e.tensor_sub(tmpd, m1v, m2v))
        VV(lambda e: e.tensor_mul(tmpd, tmpd, tmpd))
        VV(lambda e: e.tensor_scalar(tmpd, tmpd, 0.25, None, ALU.mult))
        VV(lambda e: e.tensor_add(svar, v1v, v2v))
        VV(lambda e: e.tensor_scalar(svar, svar, 0.5, None, ALU.mult))
        VV(lambda e: e.tensor_add(svar, svar, tmpd))
        vda = self.xfence("vector")
        meanv = self.STM[:, 0, :]
        varv = self.STM[:, 1, :]
        g0 = self.GSM[:, 0, :]
        g1 = self.GSM[:, 1, :]
        g2 = self.GSM[:, 2, :]
        ln = mode.startswith("ln")
        if ln:
            self.wait("scalar", "dv", vda)
            self.em("scalar", lambda e: e.activation(g1, varv, AF.Sqrt, bias=1e-5))
            vsc = self.scdrain()
            self.wait("vector", "sc", vsc)
            self.em("vector", lambda e: e.reciprocal(g0, g1))
            self.xfence("vector")
            self.em("vector", lambda e: e.tensor_scalar(g1, meanv, -1.0, None, ALU.mult))
            self.xfence("vector")
            self.em("vector", lambda e: e.tensor_mul(g2, g1, g0))
            vready = self.xfence("vector")
            self.wait("scalar", "dv", vready)
        else:
            self.em("vector", lambda e: e.tensor_mul(g1, meanv, meanv))
            self.xfence("vector")
            self.em("vector", lambda e: e.tensor_add(g1, g1, varv))
            vready = self.xfence("vector")
            self.wait("scalar", "dv", vready)
            self.em("scalar", lambda e: e.activation(g1, g1, AF.Sqrt, bias=1e-6))
            vsc = self.scdrain()
            self.wait("vector", "sc", vsc)
            self.em("vector", lambda e: e.reciprocal(g0, g1))
            vready = self.xfence("vector")
            self.wait("scalar", "dv", vready)
        gamma = {"rms1": self.G1, "rms2": self.G2}.get(mode)
        if mode == "ln_tr":
            for tt in range(NT):
                self.em("scalar", lambda e, t=tt: e.activation(
                    self.TR[:, t, :], src[:, t, :], AF.Identity,
                    scale=self.GSM[:, 0, t:t + 1], bias=self.GSM[:, 2, t:t + 1]))
            v = self.xfence("scalar")
            return ("sc", v)
        last_tp = [0, 0]
        for tt in range(NT):
            tb = tt % 2
            if tt >= 2:
                self.wait("scalar", "qw", last_tp[tb])
            if ln:
                self.em("scalar", lambda e, t=tt, b=tb: e.activation(
                    self.XTOK[:, b, :], src[:, t, :], AF.Identity,
                    scale=self.GSM[:, 0, t:t + 1], bias=self.GSM[:, 2, t:t + 1]))
            else:
                self.em("scalar", lambda e, t=tt, b=tb: e.activation(
                    self.XTOK[:, b, :], src[:, t, :], AF.Identity,
                    scale=self.GSM[:, 0, t:t + 1]))
            vsc = self.scdrain()
            if gamma is not None:
                self.wait("vector", "sc", vsc)
                self.em("vector", lambda e, b=tb: e.tensor_mul(
                    self.XTOK[:, b, :], self.XTOK[:, b, :], gamma[:, :])
                    .then_inc(self.SM["dv"], 1))
                vg = self.bump("dv")
                self.wait("sync", "dv", vg)
            else:
                self.wait("sync", "sc", vsc)
            tout = A(self.XNT, 128 * tt, [[ND * S, 128], [S, ND], [1, 128]])
            tin = A(self.XTOK, tb * D, [[2 * D, 128], [128, ND], [1, 128]])
            last_tp[tb] = self.dma("qw", tout, tin, transpose=True)
        return ("qw", max(last_tp))

    def attn(self, kind, xnt_ready, sub="full"):
        cfg = {
            "a": dict(dst=self.XR, V=self.V_B, OT=self.OT_B,
                      slabQ=self.sc_slab["aq"], slabK=self.sc_slab["ak"],
                      slabV=self.sc_slab["av"], rowO=self.sc_row["ao"],
                      bq=None, bk=None, bv=None, orow=None, rope=False),
            "inner": dict(dst=self.TR, V=self.V_A, OT=self.OT_A,
                          slabQ=self.sc_slab["fq"], slabK=self.sc_slab["fk"],
                          slabV=self.sc_slab["fv"], rowO=self.sc_row["fo"],
                          bq=self.FBQ, bk=self.FBK, bv=self.FBV,
                          orow=4096, rope=True),
            "b": dict(dst=self.XR, V=self.V_B, OT=self.OT_B,
                      slabQ=self.sc_slab["bq"], slabK=self.sc_slab["bk"],
                      slabV=self.sc_slab["bv"], rowO=self.sc_row["bo"],
                      bq=self.BQ, bk=self.BK, bv=self.BV,
                      orow=5120, rope=True),
        }[kind]
        Vb, OTb, dst = cfg["V"], cfg["OT"], cfg["dst"]
        expc = 64.0 if kind == "inner" else 32.0
        psS, psT = self.psS, self.psT
        af = getattr(self, "_arena_free", ("dv", 0))
        self.wait("gpsimd", af[0], af[1])
        arena_v = self.load_arena(kind)
        for eng in ("tensor", "scalar", "vector"):
            self.wait(eng, xnt_ready[0], xnt_ready[1])
        vload = [0, 0]
        vmm_pe = [0, 0]
        evac_sc = [0, 0]
        tdma = [0, 0]
        for oc in range(8):
            wb = oc % 2
            if oc >= 2:
                self.wait("gpsimd", "pe", vmm_pe[wb])
            vload[wb] = self.dma("qg", self.WB1[:, wb, :, :],
                                 A(cfg["slabV"], oc * SL, [[ND * 128, 128], [128, ND], [1, 128]]))
            ps = [psS, psT][wb]
            self.wait("tensor", "qg", vload[wb])
            if oc >= 2 and sub not in ("vmm",):
                self.wait("tensor", "sc", evac_sc[wb])
            for dt in range(ND):
                for qn in range(4):
                    fin = (dt == ND - 1 and qn == 3)
                    self.em("tensor", lambda e, w=wb, d=dt, q=qn, p=ps, f=fin:
                            (e.matmul(p[:, q * 512:(q + 1) * 512], self.WB1[:, w, d, :],
                                      self.XNT[:, d, q * 512:(q + 1) * 512],
                                      start=(d == 0), stop=(d == ND - 1))
                             .then_inc(self.SM["pe"], 1)) if f else
                            e.matmul(p[:, q * 512:(q + 1) * 512], self.WB1[:, w, d, :],
                                     self.XNT[:, d, q * 512:(q + 1) * 512],
                                     start=(d == 0), stop=(d == ND - 1)))
            vmm_pe[wb] = self.bump("pe")
            if sub == "vmm":
                continue
            self.wait("scalar", "pe", vmm_pe[wb])
            if oc >= 1:
                self.wait("scalar", "qw", tdma[1 - wb])
            bv = cfg["bv"]
            for hf in range(2):
                if bv is not None:
                    self.em("scalar", lambda e, p=ps, o=oc, b=bv, h=hf: e.activation(
                        self.QT[:, h * 1024:(h + 1) * 1024], p[:, h * 1024:(h + 1) * 1024],
                        AF.Identity, bias=b[:, o:o + 1]))
                else:
                    self.em("scalar", lambda e, p=ps, h=hf: e.activation(
                        self.QT[:, h * 1024:(h + 1) * 1024], p[:, h * 1024:(h + 1) * 1024],
                        AF.Identity))
            evac_sc[wb] = self.scdrain()
            if sub == "vev":
                continue
            self.wait("sync", "sc", evac_sc[wb])
            tout = A(Vb, 128 * oc, [[NT * D, 128], [D, NT], [1, 128]])
            tin = A(self.QT, 0, [[S, 128], [128, NT], [1, 128]])
            tdma[wb] = self.dma("qw", tout, tin, transpose=True)
        if sub == "vmm":
            return ("pe", max(vmm_pe))
        if sub == "vev":
            return ("sc", max(evac_sc))
        v_ready = max(tdma)
        if sub == "v":
            return ("qw", v_ready)
        self.wait("tensor", "qg", arena_v)
        self.wait("vector", "qg", arena_v)
        last_qk_pe = max(vmm_pe)
        for pr in range(NPAIR):
            self.wait("gpsimd", "pe", last_qk_pe)
            vq = self.dma("qg", self.WB1[:, 0, :, :],
                          A(cfg["slabQ"], pr * SL, [[ND * 128, 128], [128, ND], [1, 128]]))
            vk = self.dma("qg", self.WB2[:, 0, :, :],
                          A(cfg["slabK"], pr * SL, [[ND * 128, 128], [128, ND], [1, 128]]))
            self.wait("tensor", "qg", vk)
            if pr == 0:
                self.wait("tensor", "qw", v_ready)
                self.wait("tensor", "sc", max(evac_sc))
                self.wait("scalar", "qw", v_ready)
            qk_evac = []
            for which, wbuf, ps, bias in ((self.QT, self.WB1, psS, cfg["bq"]),
                                          (self.KT, self.WB2, psT, cfg["bk"])):
                for th in range(2):
                    for dt in range(ND):
                        for qn in range(2):
                            fin = (dt == ND - 1 and qn == 1)
                            self.em("tensor", lambda e, w=wbuf, d=dt, t=th, q=qn, p=ps, f=fin:
                                    (e.matmul(p[:, t * 1024 + q * 512:t * 1024 + (q + 1) * 512],
                                              w[:, 0, d, :],
                                              self.XNT[:, d, t * 1024 + q * 512:t * 1024 + (q + 1) * 512],
                                              start=(d == 0), stop=(d == ND - 1))
                                     .then_inc(self.SM["pe"], 1)) if f else
                                    e.matmul(p[:, t * 1024 + q * 512:t * 1024 + (q + 1) * 512],
                                             w[:, 0, d, :],
                                             self.XNT[:, d, t * 1024 + q * 512:t * 1024 + (q + 1) * 512],
                                             start=(d == 0), stop=(d == ND - 1)))
                    vpe = self.bump("pe")
                    self.wait("scalar", "pe", vpe)
                    if bias is not None:
                        self.em("scalar", lambda e, o=which, t=th, p=ps, bb=bias, r=pr:
                                e.activation(o[:, t * 1024:(t + 1) * 1024],
                                             p[:, t * 1024:(t + 1) * 1024],
                                             AF.Identity, bias=bb[:, r:r + 1]))
                    else:
                        self.em("scalar", lambda e, o=which, t=th, p=ps:
                                e.activation(o[:, t * 1024:(t + 1) * 1024],
                                             p[:, t * 1024:(t + 1) * 1024], AF.Identity))
                    qk_evac.append(self.scdrain())
            last_qk_pe = self.cur("pe")
            self.wait("tensor", "sc", qk_evac[-1])
            if sub == "qkv":
                continue
            if cfg["rope"]:
                rope_dvs = [0, 0]
                rope_dv = 0
                for ii, (buf, th) in enumerate([(self.QT, 0), (self.QT, 1),
                                                (self.KT, 0), (self.KT, 1)]):
                    ps = [psS, psT][ii % 2]
                    if ii >= 2:
                        self.wait("tensor", "dv", rope_dvs[ii % 2])
                    for qn in range(2):
                        self.em("tensor", lambda e, b=buf, t=th, q=qn, p=ps, f=(qn == 1):
                                (e.matmul(p[:, q * 512:(q + 1) * 512], self.CO_PERM[:, :],
                                          b[:, t * 1024 + q * 512:t * 1024 + (q + 1) * 512],
                                          start=True, stop=True)
                                 .then_inc(self.SM["pe"], 1)) if f else
                                e.matmul(p[:, q * 512:(q + 1) * 512], self.CO_PERM[:, :],
                                         b[:, t * 1024 + q * 512:t * 1024 + (q + 1) * 512],
                                         start=True, stop=True))
                    vpe = self.bump("pe")
                    self.wait("vector", "pe", vpe)
                    self.em("vector", lambda e, p=ps, t=th: e.scalar_tensor_tensor(
                        self.PB[:, 2, :], p[:, 0:1024], self.CO_RSGN[:, 0:1],
                        A(self.ARENA, 2048 + t * 1024, [[6144, 128], [1, 1024]]),
                        ALU.mult, ALU.mult))
                    self.em("vector", lambda e, b=buf, t=th: e.tensor_mul(
                        self.PB[:, 0, :], b[:, t * 1024:(t + 1) * 1024],
                        A(self.ARENA, t * 1024, [[6144, 128], [1, 1024]])))
                    self.em("vector", lambda e, b=buf, t=th: e.tensor_add(
                        b[:, t * 1024:(t + 1) * 1024], self.PB[:, 0, :], self.PB[:, 2, :])
                        .then_inc(self.SM["dv"], 1))
                    rope_dv = self.bump("dv")
                    rope_dvs[ii % 2] = rope_dv
                self.wait("tensor", "dv", rope_dv)
                self.wait("scalar", "dv", rope_dv)
            if sub == "rope":
                continue
            for qq in range(4):
                if kind == "a":
                    ks = list(range(0, 4 * qq + 4))
                elif kind == "inner":
                    ks = list(range(16))
                else:
                    ks = list(range(max(0, 4 * qq - 1), 4 * qq + 4))
                exp_sc = [0, 0]
                pv_pe = [0, 0]
                dec_dv = [0, 0]
                nks = len(ks)
                for idx, j in enumerate(ks):
                    half = idx % 2
                    soff = half * 1024
                    if idx >= 2:
                        self.wait("tensor", "sc", exp_sc[half])
                    for (a0, tp) in ((0, (0, 0)), (64, (64, 0))):
                        self.em("tensor", lambda e, a=a0, t=tp, jj=j, q=qq, o=soff, f=(a0 == 64):
                                (e.matmul(self.psS[:, o + a * 8:o + a * 8 + 512],
                                          self.KT[a:a + 64, 128 * jj:128 * jj + 128],
                                          self.QT[a:a + 64, 512 * q:512 * q + 512],
                                          start=True, stop=True, tile_position=t)
                                 .then_inc(self.SM["pe"], 1)) if f else
                                e.matmul(self.psS[:, o + a * 8:o + a * 8 + 512],
                                         self.KT[a:a + 64, 128 * jj:128 * jj + 128],
                                         self.QT[a:a + 64, 512 * q:512 * q + 512],
                                         start=True, stop=True, tile_position=t))
                    vpe = self.bump("pe")
                    self.wait("scalar", "pe", vpe)
                    if idx >= 2:
                        self.wait("scalar", "pe", pv_pe[half])
                        if kind == "a":
                            self.wait("scalar", "dv", dec_dv[half])
                    self.em("scalar", lambda e, h=half, o=soff: e.activation(
                        self.PB[:, h, :], self.psS[:, o:o + 1024], AF.Exp,
                        scale=np.float32(EXPS), bias=np.float32(-expc)))
                    exp_sc[half] = self.scdrain()
                    need_mask = (kind == "b") or (kind == "a" and j >= 4 * qq)
                    moff = 512 + 512 * qq - 128 * j
                    if need_mask:
                        base = 2560 if kind == "a" else 4096 - 128
                        self.wait("vector", "sc", exp_sc[half])
                        self.em("vector", lambda e, h=half, m=moff, bs=base: e.tensor_mul(
                            self.PB[:, h, :], self.PB[:, h, :],
                            A(self.ARENA, bs + m, [[6144, 128], [0, 2], [1, 512]]))
                            .then_inc(self.SM["dv"], 1))
                        vm = self.bump("dv")
                        self.wait("tensor", "dv", vm)
                    else:
                        self.wait("tensor", "sc", exp_sc[half])
                    for s2 in range(2):
                        self.em("tensor", lambda e, h=half, s3=s2, i=idx, f=(s2 == 1 and kind == "a"):
                                (e.matmul(
                                    self.psT[0:1, 1024 + 512 * s3:1536 + 512 * s3],
                                    self.CO_ONESC[:, 0:1], self.PB[:, h, 512 * s3:512 * s3 + 512],
                                    start=(i == 0), stop=(i == nks - 1))
                                 .then_inc(self.SM["pe"], 1)) if f else
                                e.matmul(
                                    self.psT[0:1, 1024 + 512 * s3:1536 + 512 * s3],
                                    self.CO_ONESC[:, 0:1], self.PB[:, h, 512 * s3:512 * s3 + 512],
                                    start=(i == 0), stop=(i == nks - 1)))
                    pv_src = half
                    if kind == "a":
                        vl = self.bump("pe")
                        self.wait("vector", "pe", vl)
                        self.em("vector", lambda e, h=half, m=moff: e.tensor_mul(
                            self.PB[:, 2, :], self.PB[:, h, :],
                            A(self.ARENA, m, [[6144, 128], [0, 2], [1, 512]]))
                            .then_inc(self.SM["dv"], 1))
                        dec_dv[half] = self.bump("dv")
                        self.wait("tensor", "dv", dec_dv[half])
                        pv_src = 2
                    for cg in (0, 64):
                        self.em("tensor", lambda e, c=cg, h=pv_src, jj=j, i=idx, r=pr, f=(cg == 64):
                                (e.matmul(self.psT[c:c + 64, 0:512],
                                          Vb[:, jj, 128 * r + c:128 * r + c + 64],
                                          self.PB[:, h, c * 8:c * 8 + 512],
                                          start=(i == 0), stop=(i == nks - 1),
                                          tile_position=(0, c))
                                 .then_inc(self.SM["pe"], 1)) if f else
                                e.matmul(self.psT[c:c + 64, 0:512],
                                         Vb[:, jj, 128 * r + c:128 * r + c + 64],
                                         self.PB[:, h, c * 8:c * 8 + 512],
                                         start=(i == 0), stop=(i == nks - 1),
                                         tile_position=(0, c)))
                    pv_pe[half] = self.bump("pe")
                self.wait("vector", "pe", self.cur("pe"))
                self.em("vector", lambda e: e.reciprocal(self.RR[0:1, :],
                                                         self.psT[0:1, 1024:2048])
                        .then_inc(self.SM["dv"], 1))
                self.bump("dv")
                vr = self.xfence("vector")
                self.wait("tensor", "dv", vr)
                for s2 in range(2):
                    self.em("tensor", lambda e, s3=s2, f=(s2 == 1): (e.matmul(
                        self.psT[64 * s3:64 * s3 + 64, 512:1024],
                        self.CO_ONES1F[0:1, 0:64], self.RR[0:1, 512 * s3:512 * s3 + 512],
                        start=True, stop=True).then_inc(self.SM["pe"], 1)) if f else e.matmul(
                        self.psT[64 * s3:64 * s3 + 64, 512:1024],
                        self.CO_ONES1F[0:1, 0:64], self.RR[0:1, 512 * s3:512 * s3 + 512],
                        start=True, stop=True))
                vpe = self.bump("pe")
                self.wait("scalar", "pe", vpe)
                self.em("scalar", lambda e: e.activation(self.RLB[:, :], self.psT[:, 512:1024],
                                                         AF.Identity))
                vsc = self.scdrain()
                self.wait("vector", "sc", vsc)
                self.em("vector", lambda e, r=pr, q=qq, o=OTb: e.tensor_mul(
                    o[:, r, 512 * q:512 * q + 512], self.psT[:, 0:512], self.RLB[:, :])
                    .then_inc(self.SM["dv"], 1))
                votq = self.bump("dv")
                self.wait("tensor", "dv", votq)
        ot_done = self.cur("dv")
        if sub in ("qkv", "rope"):
            return ("dv", self.cur("dv"))
        if sub == "sloop":
            return ("dv", ot_done)
        self.wait("gpsimd", "dv", ot_done)
        opl = 0
        for pr2 in range(NPAIR):
            opl = self.dma("qg", A(self.OPW, pr2 * 1024, [[8 * 1024, 128], [1, 1024]]),
                           A(cfg["rowO"], pr2 * 128 * D, [[D, 128], [1, D]]))
        self.wait("tensor", "qg", opl)
        ev_dv = [0, 0]
        for tc in range(NT):
            half = tc % 2
            ps = [psS, psT][half]
            if tc >= 2:
                self.wait("tensor", "dv", ev_dv[half])
            last_is_mm = cfg["orow"] is None
            for pr2 in range(NPAIR):
                for qn in range(2):
                    fin = last_is_mm and pr2 == NPAIR - 1 and qn == 1
                    self.em("tensor", lambda e, r=pr2, t=tc, q=qn, p=ps, o=OTb, f=fin:
                            (e.matmul(p[:, q * 512:(q + 1) * 512],
                                      o[:, r, 128 * t:128 * t + 128],
                                      A(self.OPW, r * 1024 + q * 512, [[8 * 1024, 128], [1, 512]]),
                                      start=(r == 0),
                                      stop=(r == NPAIR - 1 and last_is_mm))
                             .then_inc(self.SM["pe"], 1)) if f else
                            e.matmul(p[:, q * 512:(q + 1) * 512],
                                     o[:, r, 128 * t:128 * t + 128],
                                     A(self.OPW, r * 1024 + q * 512, [[8 * 1024, 128], [1, 512]]),
                                     start=(r == 0),
                                     stop=(r == NPAIR - 1 and last_is_mm)))
            if cfg["orow"] is not None:
                ro = cfg["orow"]
                for qn in range(2):
                    self.em("tensor", lambda e, q=qn, p=ps, r0=ro, f=(qn == 1): (e.matmul(
                        p[:, q * 512:(q + 1) * 512], self.CO_ONES1[0:1, :],
                        A(self.ARENA, r0 + q * 512, [[6144, 1], [1, 512]]),
                        start=False, stop=(q == 1)).then_inc(self.SM["pe"], 1)) if f else e.matmul(
                        p[:, q * 512:(q + 1) * 512], self.CO_ONES1[0:1, :],
                        A(self.ARENA, r0 + q * 512, [[6144, 1], [1, 512]]),
                        start=False, stop=(q == 1)))
            vpe = self.bump("pe")
            self.wait("vector", "pe", vpe)
            self.em("vector", lambda e, t=tc, p=ps, d=dst: e.tensor_add(
                d[:, t, :], d[:, t, :], p[:, 0:1024]).then_inc(self.SM["dv"], 1))
            ev_dv[half] = self.bump("dv")
        self._arena_free = ("dv", self.cur("dv"))
        return ("dv", self.cur("dv"))
    def moe(self, xnt_ready, x_dep):
        psS, psT = self.psS, self.psT
        gl = self.dma("qg", self.GATET[:, :, :],
                      A(self.sc_row["gate"], 0, [[16, 128], [128 * 16, ND], [1, 16]]))
        self.wait("gpsimd", xnt_ready[0], xnt_ready[1])
        idv = self.dma("qg", self.CO_IDENTF[:, :], self.cin["IDENTF"].ap())
        for eng in ("tensor", "vector"):
            self.wait(eng, xnt_ready[0], xnt_ready[1])
        self.wait("tensor", "qg", idv)
        self.wait("tensor", "qg", gl)
        cp_dv = [0, 0]
        for tt in range(NT):
            pz = tt % 2
            col = 512 * pz
            if tt >= 2:
                self.wait("tensor", "dv", cp_dv[pz])
            for dt in range(ND):
                self.em("tensor", lambda e, d=dt, t=tt, c=col, f=(dt == ND - 1): (e.matmul(
                    psT[:, c:c + 8], self.XNT[:, d, 128 * t:128 * t + 128],
                    self.GATET[:, d, 0:8], start=(d == 0), stop=(d == ND - 1))
                    .then_inc(self.SM["pe"], 1)) if f else e.matmul(
                    psT[:, c:c + 8], self.XNT[:, d, 128 * t:128 * t + 128],
                    self.GATET[:, d, 0:8], start=(d == 0), stop=(d == ND - 1)))
            vpe = self.bump("pe")
            self.wait("vector", "pe", vpe)
            self.em("vector", lambda e, t=tt, c=col: e.tensor_copy(
                self.GA[:, t, :], psT[:, c:c + 8]).then_inc(self.SM["dv"], 1))
            cp_dv[pz] = self.bump("dv")
        self.xfence("vector")
        m1 = self.GSM[:, 0, :]
        m2 = self.GSM[:, 1, :]
        w1g = self.GSM[:, 2, :]
        w2g = self.GSM[:, 3, :]
        m1b = A(self.GSM, 0 * NT, [[NT * 4, 128], [1, NT], [0, E]])
        m2b = A(self.GSM, 1 * NT, [[NT * 4, 128], [1, NT], [0, E]])
        w1b = A(self.GSM, 2 * NT, [[NT * 4, 128], [1, NT], [0, E]])
        w2b = A(self.GSM, 3 * NT, [[NT * 4, 128], [1, NT], [0, E]])
        ga, gb, gc = self.GA, self.GB, self.GC

        def V(f):
            self.em("vector", f)
            self.xfence("vector")

        V(lambda e: e.reduce_max(m1, ga[:, :, :], AX.X))
        V(lambda e: e.tensor_tensor(gb[:, :, :], ga[:, :, :], m1b, ALU.is_equal))
        V(lambda e: e.scalar_tensor_tensor(gc[:, :, :], gb[:, :, :], -1e30, ga[:, :, :],
                                           ALU.mult, ALU.add))
        V(lambda e: e.reduce_max(m2, gc[:, :, :], AX.X))
        V(lambda e: e.tensor_tensor(w1g, m1, m2, ALU.subtract))
        vdv = self.cur("dv")
        self.wait("scalar", "dv", vdv)
        self.em("scalar", lambda e: e.activation(w2g, w1g, AF.Exp, scale=-1.0)
                .then_inc(self.SM["sc"], 1))
        self.bump("sc")
        vsc = self.xfence("scalar")
        self.wait("vector", "sc", vsc)
        V(lambda e: e.tensor_scalar(w2g, w2g, 1.0, None, ALU.add))
        V(lambda e: e.reciprocal(w1g, w2g))
        V(lambda e: e.tensor_scalar(w2g, w1g, -1.0, 1.0, ALU.mult, ALU.add))
        V(lambda e: e.tensor_tensor(ga[:, :, :], gc[:, :, :], m2b, ALU.is_equal))
        V(lambda e: e.tensor_mul(gb[:, :, :], gb[:, :, :], w1b))
        V(lambda e: e.tensor_mul(ga[:, :, :], ga[:, :, :], w2b))
        V(lambda e: e.tensor_add(gc[:, :, :], ga[:, :, :], gb[:, :, :]))
        comb_rdy = self.cur("dv")
        self.wait("vector", x_dep[0], x_dep[1])
        self.wait("tensor", "dv", comb_rdy)
        ring = [psS, psT]
        rel = [("dv", 0), ("dv", 0)]
        it = [0]

        def slot():
            z = it[0] % 2
            sem, v = rel[z]
            self.wait("tensor", sem, v)
            it[0] += 1
            return z, ring[z]

        wuse1 = [0, 0]
        wuse3 = [0, 0]
        s1_dv = 0
        vh_last = 0
        prev_p2_pe = 0
        ytok_dv = 0
        for e_ in range(E):
            self.wait("gpsimd", "pe", prev_p2_pe)
            b3v = self.dma("qg", self.B3ROW[0:1, :],
                           A(self.vins["b3_r"], e_ * D, [[D, 1], [1, D]]))
            z, ps = slot()
            crow_sc = [0, 0, 0, 0]
            for tt in range(NT):
                bank = tt % 4
                if tt >= 4:
                    self.wait("tensor", "dv", crow_sc[bank])
                self.em("tensor", lambda e, t=tt, bk=bank, p=ps, ee=e_: e.transpose(
                    p[0:1, 512 * bk:512 * bk + 128], self.GC[:, t, ee:ee + 1],
                    self.CO_IDENTF[:, :]).then_inc(self.SM["pe"], 1))
                vpe = self.bump("pe")
                self.wait("vector", "pe", vpe)
                self.em("vector", lambda e, t=tt, bk=bank, p=ps: e.tensor_copy(
                    self.CROW[0:1, 128 * t:128 * t + 128],
                    p[0:1, 512 * bk:512 * bk + 128])
                    .then_inc(self.SM["dv"], 1))
                crow_sc[bank] = self.bump("dv")
            self.wait("tensor", "dv", max(crow_sc))
            for qn in range(4):
                self.em("tensor", lambda e, q=qn, p=ps, f=(qn == 3): (e.matmul(
                    p[:, q * 512:(q + 1) * 512], self.CO_ONES1[0:1, :],
                    self.CROW[0:1, q * 512:(q + 1) * 512], start=True, stop=True)
                    .then_inc(self.SM["pe"], 1)) if f else e.matmul(
                    p[:, q * 512:(q + 1) * 512], self.CO_ONES1[0:1, :],
                    self.CROW[0:1, q * 512:(q + 1) * 512], start=True, stop=True))
            vpe = self.bump("pe")
            self.wait("scalar", "pe", vpe)
            self.wait("scalar", "dv", vh_last)
            for hf in range(2):
                self.em("scalar", lambda e, h=hf, p=ps: e.activation(
                    self.CMB[:, h * 1024:(h + 1) * 1024], p[:, h * 1024:(h + 1) * 1024],
                    AF.Identity))
            vcmb = self.scdrain()
            rel[z] = ("sc", vcmb)
            self.wait("vector", "sc", vcmb)
            for hc in range(NHT):
                wz = hc % 2
                if hc >= 2 or e_ > 0:
                    self.wait("gpsimd", "pe", wuse1[wz])
                v1 = self.dma("qg", self.WB1[:, wz, :, :],
                              A(self.sc_slab["w1"], (e_ * NHT + hc) * SL,
                                [[ND * 128, 128], [128, ND], [1, 128]]))
                v2 = self.dma("qg", self.WB2[:, wz, :, :],
                              A(self.sc_slab["w2"], (e_ * NHT + hc) * SL,
                                [[ND * 128, 128], [128, ND], [1, 128]]))
                self.wait("tensor", "qg", v2)
                for th in range(2):
                    z, ps = slot()
                    for dt in range(ND):
                        for qn in range(2):
                            self.em("tensor", lambda e, w=wz, d=dt, t=th, q=qn, p=ps:
                                    e.matmul(p[:, q * 512:(q + 1) * 512],
                                             self.WB1[:, w, d, :],
                                             self.XNT[:, d, t * 1024 + q * 512:t * 1024 + (q + 1) * 512],
                                             start=(d == 0), stop=(d == ND - 1)))
                    for dt in range(ND):
                        for qn in range(2):
                            fin = (dt == ND - 1 and qn == 1)
                            self.em("tensor", lambda e, w=wz, d=dt, t=th, q=qn, p=ps, f=fin:
                                    (e.matmul(p[:, 1024 + q * 512:1024 + (q + 1) * 512],
                                              self.WB2[:, w, d, :],
                                              self.XNT[:, d, t * 1024 + q * 512:t * 1024 + (q + 1) * 512],
                                              start=(d == 0), stop=(d == ND - 1))
                                     .then_inc(self.SM["pe"], 1)) if f else
                                    e.matmul(p[:, 1024 + q * 512:1024 + (q + 1) * 512],
                                             self.WB2[:, w, d, :],
                                             self.XNT[:, d, t * 1024 + q * 512:t * 1024 + (q + 1) * 512],
                                             start=(d == 0), stop=(d == ND - 1)))
                    vpe = self.bump("pe")
                    wuse1[wz] = vpe
                    self.wait("scalar", "pe", vpe)
                    self.wait("scalar", "dv", s1_dv)
                    self.em("scalar", lambda e, p=ps, ee=e_, h=hc: e.activation(
                        self.S1[:, :], p[:, 0:1024], AF.Silu,
                        bias=self.B1B[:, ee * NHT + h:ee * NHT + h + 1]))
                    vs = self.scdrain()
                    self.wait("vector", "sc", vs)
                    self.em("vector", lambda e, p=ps, ee=e_, h=hc, t=th:
                            e.scalar_tensor_tensor(
                                self.HT[:, h, t * 1024:(t + 1) * 1024], p[:, 1024:2048],
                                self.B2B[:, ee * NHT + h:ee * NHT + h + 1], self.S1[:, :],
                                ALU.add, ALU.mult))
                    self.em("vector", lambda e, h=hc, t=th: e.tensor_mul(
                        self.HT[:, h, t * 1024:(t + 1) * 1024],
                        self.HT[:, h, t * 1024:(t + 1) * 1024],
                        self.CMB[:, t * 1024:(t + 1) * 1024]).then_inc(self.SM["dv"], 1))
                    vh = self.bump("dv")
                    vh_last = vh
                    s1_dv = vh
                    rel[z] = ("dv", vh)
            self.wait("tensor", "qg", b3v)
            self.wait("tensor", "dv", vh_last)
            for oc in range(ND):
                wz3 = oc % 2
                if oc >= 2 or e_ > 0:
                    self.wait("gpsimd", "pe", wuse3[wz3])
                v3 = self.dma("qg", self.WB3[:, wz3, :, :],
                              A(self.sc_slab["w3"], (e_ * ND + oc) * 128 * NHT * 128,
                                [[NHT * 128, 128], [128, NHT], [1, 128]]))
                z, ps = slot()
                self.wait("tensor", "qg", v3)
                for ht in range(NHT):
                    for qn in range(4):
                        self.em("tensor", lambda e, w=wz3, hh=ht, q=qn, p=ps:
                                e.matmul(p[:, q * 512:(q + 1) * 512],
                                         self.WB3[:, w, hh, :],
                                         self.HT[:, hh, q * 512:(q + 1) * 512],
                                         start=(hh == 0), stop=False))
                for qn in range(4):
                    self.em("tensor", lambda e, q=qn, p=ps, o=oc, f=(qn == 3): (e.matmul(
                        p[:, q * 512:(q + 1) * 512], self.B3ROW[0:1, 128 * o:128 * o + 128],
                        self.CMB[0:1, q * 512:(q + 1) * 512], start=False, stop=True)
                        .then_inc(self.SM["pe"], 1)) if f else e.matmul(
                        p[:, q * 512:(q + 1) * 512], self.B3ROW[0:1, 128 * o:128 * o + 128],
                        self.CMB[0:1, q * 512:(q + 1) * 512], start=False, stop=True))
                vpe = self.bump("pe")
                wuse3[wz3] = vpe
                prev_p2_pe = vpe
                self.wait("scalar", "pe", vpe)
                if self._ybuf_qw:
                    self.wait("scalar", "qw", self._ybuf_qw)
                for hf in range(2):
                    self.em("scalar", lambda e, p=ps, h=hf: e.activation(
                        self.YBUF[:, h * 1024:(h + 1) * 1024], p[:, h * 1024:(h + 1) * 1024],
                        AF.Identity))
                vy = self.scdrain()
                rel[z] = ("sc", vy)
                self.wait("sync", "sc", vy)
                self.wait("sync", "dv", ytok_dv)
                tout = A(self.YTOK, 0, [[NT * 128, 128], [128, NT], [1, 128]])
                tin = A(self.YBUF, 0, [[S, 128], [128, NT], [1, 128]])
                self._ybuf_qw = self.dma("qw", tout, tin, transpose=True)
                self.wait("vector", "qw", self._ybuf_qw)
                self.em("vector", lambda e, o=oc: e.tensor_add(
                    A(self.XR, 128 * o, [[NT * D, 128], [D, NT], [1, 128]]),
                    A(self.XR, 128 * o, [[NT * D, 128], [D, NT], [1, 128]]),
                    self.YTOK[:, :, :]).then_inc(self.SM["dv"], 1))
                ytok_dv = self.bump("dv")
        self._arena_free = ("dv", self.cur("dv"))
        return ("dv", self.cur("dv"))

    def ffn(self, b, dep=None):
        sub = getattr(self, "sub", "full")
        r1 = self.norm_pass(self.XR, "ln_tr", dep=dep)
        if sub == "fln":
            return r1
        if dep is not None:
            self.wait("gpsimd", dep[0], dep[1])
        sp = self.spill_x()
        r2 = self.norm_pass(self.TR, "rms1", dep=r1)
        if sub == "frms1":
            return r2
        self.wait("sync", "qo", sp)
        ra = self.attn("inner", r2, sub={"finq": "qkv", "finr": "rope"}.get(sub, "full"))
        if sub in ("fattn", "finq", "finr"):
            return ra
        r3 = self.norm_pass(self.TR, "rms2", dep=ra)
        self.wait("gpsimd", ra[0], ra[1])
        rl = self.reload_x()
        self.wait("vector", "qg", rl)
        self.wait("vector", ra[0], ra[1])
        for tt in range(NT):
            self.em("vector", lambda e, t=tt: e.tensor_add(
                self.XR[:, t, :], self.XR[:, t, :], self.TR[:, t, :])
                .then_inc(self.SM["dv"], 1))
            self.bump("dv")
        xt = ("dv", self.cur("dv"))
        if sub == "fxt":
            return xt
        return self.moe(r3, xt)

    def module(self, b):
        lx = self.load_x(b)
        dep = ("qg", lx)
        if self.upto >= 1:
            r = self.norm_pass(self.XR, "ln_xnt", dep=dep)
            s_ = getattr(self, "sub", "full")
            if s_ == "norm":
                dep = r
            else:
                dep = self.attn("a", r, sub=s_ if s_ in ("v", "vev", "vmm", "qkv", "sloop") else "full")
        if self.upto >= 2:
            dep = self.ffn(b, dep)
        if self.upto >= 3:
            r = self.norm_pass(self.XR, "ln_xnt", dep=dep)
            dep = self.attn("b", r)
        if self.upto >= 4:
            dep = self.ffn(b, dep)
        dmp = getattr(self, "dump", None)
        if dmp is not None and b == 0:
            self.wait("gpsimd", dep[0], dep[1])
            self.barrier()
            doff = 0
            for nm in dmp.split(","):
                buf = getattr(self, nm)
                n = 1
                for s_ in buf.shape:
                    n *= s_
                if buf.shape[0] == 1:
                    self.dma("qo", A(self.xout, S * D + doff, [[n, 1], [1, n]]),
                             A(buf, 0, [[n, 1], [1, n]]))
                else:
                    free = min(n // 128, (S * D - doff) // 128)
                    self.dma("qo", A(self.xout, S * D + doff, [[free, 128], [1, free]]),
                             A(buf, 0, [[free, 128], [1, free]]))
                doff += n
        self.wait("gpsimd", dep[0], dep[1])
        self.barrier()
        return self.store_x(b)

    def build_all(self):
        self._ybuf_qw = 0
        cvals = [1e-5, 1e-6]
        for i, cvv in enumerate(cvals):
            self.em("vector", lambda e, ii=i, vv2=cvv: e.memset(self.CPOOL[:, ii:ii + 1], vv2))
            self.nc.const_aps.register(F32, cvv, self.CPOOL[:, i:i + 1])
        self.em("vector", lambda e: e.nop().then_inc(self.SM["dv"], 1))
        cb = self.bump("dv")
        for eng in ("scalar", "tensor", "gpsimd", "sync"):
            self.wait(eng, "dv", cb)
        cl = self.prep_const_loads()
        if self.upto >= 1:
            self.prep_weights()
        nb = 1 if getattr(self, "dump", None) else 2
        for b in range(nb):
            self.module(b)
        self.wait("gpsimd", "qo", self.cur("qo"))
        return self.finalize()


_UPTO = 4


def kernel(**inputs):
    cons = host_constants()
    i = inputs
    qkvb = np.asarray(i["L0_ffn_qkv_b"])
    common = {
        "aW": np.asarray(i["L0_attn_a_W_attn"]), "aP": np.asarray(i["L0_attn_a_W_proj"]),
        "bWq": np.asarray(i["L0_attn_b_Wq"]), "bWk": np.asarray(i["L0_attn_b_Wk"]),
        "bWv": np.asarray(i["L0_attn_b_Wv"]), "bWo": np.asarray(i["L0_attn_b_Wo"]),
        "fqkv": np.asarray(i["L0_ffn_qkv_w"]), "fout": np.asarray(i["L0_ffn_out_w"]),
        "fgate": np.asarray(i["L0_ffn_gate_w"]),
        "fw1": np.asarray(i["L0_ffn_w1"]), "fw2": np.asarray(i["L0_ffn_w2"]),
        "fw3": np.asarray(i["L0_ffn_w3"]),
        "bq_pp": _pp(np.asarray(i["L0_attn_b_bq"])), "bk_pp": _pp(np.asarray(i["L0_attn_b_bk"])),
        "bv_pp": _pp(np.asarray(i["L0_attn_b_bv"])),
        "fbq_pp": _pp(qkvb[0:D]), "fbk_pp": _pp(qkvb[D:2 * D]), "fbv_pp": _pp(qkvb[2 * D:3 * D]),
        "b1_pp": np.ascontiguousarray(
            np.asarray(i["L0_ffn_b1"]).reshape(E, NHT, 128).transpose(2, 0, 1).reshape(128, E * NHT)),
        "b2_pp": np.ascontiguousarray(
            np.asarray(i["L0_ffn_b2"]).reshape(E, NHT, 128).transpose(2, 0, 1).reshape(128, E * NHT)),
        "bo_row": np.ascontiguousarray(np.asarray(i["L0_attn_b_bo"])[None, :]),
        "fob_row": np.ascontiguousarray(np.asarray(i["L0_ffn_out_b"])[None, :]),
        "b3_r": np.asarray(i["L0_ffn_b3"]),
        "g1_rep": np.ascontiguousarray(
            np.broadcast_to(np.asarray(i["L0_ffn_norm1_w"]), (128, D))),
        "g2_rep": np.ascontiguousarray(
            np.broadcast_to(np.asarray(i["L0_ffn_norm2_w"]), (128, D))),
    }
    for k, v in cons.items():
        common["c_" + k] = v
    x = np.asarray(i["L0_attn_a_x"])
    bld = Build(upto=_UPTO)
    nc = bld.build_all()
    in_maps = []
    for core in range(8):
        m = dict(common)
        m["xin"] = np.ascontiguousarray(x[2 * core:2 * core + 2])
        in_maps.append(m)
    res = run_bass_kernel_spmd(nc, in_maps, core_ids=list(range(8)))
    out = np.concatenate([r["xout"] for r in res.results], axis=0)
    return out.astype(np.float32)


# version 02 (final)
# speedup vs baseline: 1.0507x; 1.0507x over previous
import math
from contextlib import ExitStack

import numpy as np
import ml_dtypes

import concourse.bass as bass
import concourse.mybir as mybir
from concourse.bass_utils import run_bass_kernel_spmd

F32 = mybir.dt.float32
BF16 = mybir.dt.bfloat16
AF = mybir.ActivationFunctionType
ALU = mybir.AluOpType
AX = mybir.AxisListType
BF = ml_dtypes.bfloat16

D, S, NT, ND = 1024, 2048, 16, 8
H, HD, NPAIR = 16, 64, 8
E, HH, NHT = 8, 2048, 16
KBy = 1024
EXPS = 0.125
ENGS = ("tensor", "vector", "scalar", "gpsimd", "sync")
SL = 128 * ND * 128


def A(t, off, dims):
    if isinstance(t, bass.AP):
        t = t.tensor
    return bass.AP(t, off, [list(d) for d in dims])


def host_constants():
    c = {}
    dl = 2048 - 16 + 1
    nums = np.linspace(0.0, 1.0, dl, dtype=np.float64)
    dvv = 1.0 - np.power(nums, 1.0 / np.e)
    dvv = np.concatenate([np.ones(15), dvv])[:2048]
    kk, mm = np.meshgrid(np.arange(128), np.arange(2560), indexing="ij")
    delta = (mm - 512) - kk
    c["DSTRIP"] = np.where((delta >= 0) & (delta < 2048),
                           dvv[np.clip(delta, 0, 2047)], 0.0).astype(BF)
    kk, mm = np.meshgrid(np.arange(128), np.arange(1024), indexing="ij")
    c["CSTR"] = (((mm - 512) - kk) >= 0).astype(BF)
    kk, mm = np.meshgrid(np.arange(128), np.arange(1024), indexing="ij")
    d2 = (mm + 128 - 512) - kk
    c["WSTR"] = ((d2 >= 0) & (d2 <= 128)).astype(BF)
    inv = 1.0 / 10000.0 ** (np.arange(32, dtype=np.float64) / 32.0)
    ang = np.outer(inv, np.arange(S, dtype=np.float64))
    c["COSF"] = np.tile(np.cos(ang), (4, 1)).astype(BF)
    c["SINF"] = np.tile(np.sin(ang), (4, 1)).astype(BF)
    c["RSGN"] = np.tile(np.r_[-np.ones(32), np.ones(32)], 2).astype(np.float32).reshape(128, 1)
    swap = np.arange(128)
    swap = (swap // 64) * 64 + ((swap + 32) % 64)
    perm = np.zeros((128, 128), dtype=BF)
    perm[swap, np.arange(128)] = 1.0
    c["PERM"] = perm
    c["ONES1"] = np.ones((1, 128), dtype=BF)
    c["IDENTF"] = np.eye(128, dtype=np.float32)
    c["ONES1F"] = np.ones((1, 128), dtype=np.float32)
    c["ONESC"] = np.ones((128, 1), dtype=BF)
    return c


def _pp(b):
    n = b.shape[0] // 128
    return np.ascontiguousarray(b.reshape(n, 128).T)


class Build:
    def __init__(self, upto=4):
        self.upto = upto
        self.nc = bass.Bass("TRN2", target_bir_lowering=False)
        self.q = {k: [] for k in ENGS}
        self.c = {}
        self.es = ExitStack()
        self._mk_io()
        self._mk_sbuf()
        self.psS = self.nc.alloc_psum_tensor("psS", [128, 2048], F32)
        self.psT = self.nc.alloc_psum_tensor("psT", [128, 2048], F32)
        self.SM = {}
        for name in ("qw", "qg", "qo", "pe", "dv", "sc", "gp", "sy"):
            self.SM[name] = self.es.enter_context(self.nc.semaphore(name))
            self.c[name] = 0

    def em(self, eng, fn):
        self.q[eng].append(fn)

    def bump(self, sem, k=1):
        self.c[sem] += k
        return self.c[sem]

    def cur(self, sem):
        return self.c[sem]

    def wait(self, eng, sem, val):
        if val > 0:
            self.em(eng, lambda e, s=self.SM[sem], v=val: e.wait_ge(s, v))

    def barrier(self):
        vals = {k: self.cur(k) for k in self.SM}
        own = {"tensor": "pe", "vector": "dv", "scalar": "sc", "gpsimd": "gp", "sync": "sy"}
        for eng in ENGS:
            for sem, v in vals.items():
                if v > 0 and sem != own.get(eng):
                    self.wait(eng, sem, v)

    def xfence(self, eng):
        sem = {"tensor": "pe", "vector": "dv", "scalar": "sc", "gpsimd": "gp"}[eng]
        if eng == "scalar":
            v1 = self.bump(sem)
            self.em(eng, lambda e, s=self.SM[sem]: e.drain().then_inc(s, 1))
            return v1
        v1 = self.bump(sem)
        self.em(eng, lambda e, s=self.SM[sem]: e.engine_nop().then_inc(s, 1)
                if hasattr(e, "engine_nop") else e.nop().then_inc(s, 1))
        self.wait(eng, sem, v1)
        v2 = self.bump(sem)
        self.em(eng, lambda e, s=self.SM[sem]: e.nop().then_inc(s, 1))
        return v2

    def scdrain(self):
        v = self.bump("sc")
        self.em("scalar", lambda e, s=self.SM["sc"]: e.drain().then_inc(s, 1))
        return v

    def dma(self, queue, dst, src, transpose=False):
        eng = "sync" if queue == "qw" else "gpsimd"
        v = self.bump(queue, 16)
        s = self.SM[queue]
        if transpose:
            self.em(eng, lambda e, d=dst, q=src: e.dma_start_transpose(d, q).then_inc(s, 16))
        else:
            self.em(eng, lambda e, d=dst, q=src: e.dma_start(d, q).then_inc(s, 16))
        return v

    def _mk_io(self):
        nc = self.nc
        dt = nc.dram_tensor
        self.xin = dt("xin", [2, S, D], F32, kind="ExternalInput")
        self.xout = dt("xout", [2, S, D], F32, kind="ExternalOutput")
        w = {}
        w["aW"] = dt("aW", [D, 3 * D], F32, kind="ExternalInput")
        w["aP"] = dt("aP", [D, D], F32, kind="ExternalInput")
        for nm in ("bWq", "bWk", "bWv", "bWo"):
            w[nm] = dt(nm, [D, D], F32, kind="ExternalInput")
        w["fqkv"] = dt("fqkv", [3 * D, D], F32, kind="ExternalInput")
        w["fout"] = dt("fout", [D, D], F32, kind="ExternalInput")
        w["fgate"] = dt("fgate", [E, D], F32, kind="ExternalInput")
        w["fw1"] = dt("fw1", [E, HH, D], F32, kind="ExternalInput")
        w["fw2"] = dt("fw2", [E, HH, D], F32, kind="ExternalInput")
        w["fw3"] = dt("fw3", [E, D, HH], F32, kind="ExternalInput")
        self.w = w
        v = {}
        for nm, sh, ddt in [("bq_pp", [128, 8], F32), ("bk_pp", [128, 8], F32),
                            ("bv_pp", [128, 8], F32), ("fbq_pp", [128, 8], F32),
                            ("fbk_pp", [128, 8], F32), ("fbv_pp", [128, 8], F32),
                            ("b1_pp", [128, E * NHT], F32), ("b2_pp", [128, E * NHT], F32),
                            ("bo_row", [1, D], F32), ("fob_row", [1, D], F32),
                            ("b3_r", [E, D], F32),
                            ("g1_rep", [128, D], F32), ("g2_rep", [128, D], F32)]:
            v[nm] = dt(nm, sh, ddt, kind="ExternalInput")
        self.vins = v
        cshapes = dict(DSTRIP=[128, 2560], CSTR=[128, 1024], WSTR=[128, 1024],
                       COSF=[128, S], SINF=[128, S], PERM=[128, 128],
                       ONES1=[1, 128], ONESC=[128, 1])
        self.cin = {k: dt("c_" + k, sh, BF16, kind="ExternalInput") for k, sh in cshapes.items()}
        self.cin["ONES1F"] = dt("c_ONES1F", [1, 128], F32, kind="ExternalInput")
        self.cin["IDENTF"] = dt("c_IDENTF", [128, 128], F32, kind="ExternalInput")
        self.cin["RSGN"] = dt("c_RSGN", [128, 1], F32, kind="ExternalInput")
        self.sc_slab = {
            "fq": dt("sl_fq", [8 * SL], BF16), "fk": dt("sl_fk", [8 * SL], BF16),
            "bq": dt("sl_bq", [8 * SL], BF16), "bk": dt("sl_bk", [8 * SL], BF16),
            "aq": dt("sl_aq", [8 * SL], BF16), "ak": dt("sl_ak", [8 * SL], BF16),
            "w1": dt("sl_w1", [E * NHT * SL], BF16), "w2": dt("sl_w2", [E * NHT * SL], BF16),
            "w3": dt("sl_w3", [E * ND * 128 * NHT * 128], BF16),
        }
        for nm in ("fv", "bv", "av"):
            self.sc_slab[nm] = dt("sl_" + nm, [8 * SL], BF16)
        self.sc_row = {
            "fo": dt("ro_fo", [D, D], BF16),
            "bo": dt("ro_bo", [D, D], BF16), "ao": dt("ro_ao", [D, D], BF16),
            "gate": dt("ro_gate", [D, 16], BF16),
        }
        self.xspill = dt("xspill", [S, D], F32)

    def _mk_sbuf(self):
        BASE = (self.nc.sbuf_base + 255) // 256 * 256
        assert BASE <= 17 * KBy, f"unexpected sbuf_base {self.nc.sbuf_base}"
        self._sbuf_BASE = BASE
        _at0 = self.nc.alloc_sbuf_tensor_at

        def at(name, shape, dtype, offset=0):
            return _at0(name, shape, dtype, offset=offset + BASE)
        self.XR = at("XR", [128, NT, D], F32, offset=0)
        self.V_A = at("V_A", [128, NT, D], BF16, offset=0)
        self.OT_A = at("OT_A", [128, NPAIR, S], BF16, offset=32 * KBy)
        self.PSTG = at("PSTG", [128, 8, 3072], BF16, offset=0)
        self.TR = at("TR", [128, NT, D], F32, offset=64 * KBy)
        self.HT = at("HT", [128, NHT, S], BF16, offset=64 * KBy)
        self.V_B = at("V_B", [128, NT, D], BF16, offset=64 * KBy)
        self.OT_B = at("OT_B", [128, NPAIR, S], BF16, offset=96 * KBy)
        self.TSTG = at("TSTG", [128, 192, 128], BF16, offset=64 * KBy)
        self.XNT = at("XNT", [128, ND, S], BF16, offset=128 * KBy)
        o = [160 * KBy]

        def nx(name, shape, dtype, nbytes, share=None):
            t = at(name, shape, dtype, offset=o[0] if share is None else share)
            if share is None:
                self._loff = o[0]
                o[0] += nbytes
            return t

        self.PB = nx("PB", [128, 3, KBy], BF16, 6 * KBy)
        po = self._loff
        self.XTOK = nx("XTOK", [128, 2, D], BF16, 0, share=po)
        self.B3ROW = nx("B3ROW", [1, D], BF16, 0, share=po)
        self.CMB = nx("CMB", [128, S], BF16, 0, share=po + 2 * KBy)
        self.QT = nx("QT", [128, S], BF16, 4 * KBy)
        qo = self._loff
        self.KT = nx("KT", [128, S], BF16, 4 * KBy)
        self.OPW = nx("OPW", [128, 8, 1024], BF16, 0, share=qo)
        self.YBUF = nx("YBUF", [128, S], BF16, 0, share=qo)
        self.YTOK = nx("YTOK", [128, NT, 128], BF16, 0, share=qo + 4 * KBy)
        self.RR = nx("RR", [1, KBy], F32, 4 * KBy)
        self.S1 = nx("S1", [128, KBy], BF16, 0, share=self._loff)
        self.CROW = nx("CROW", [1, S], BF16, 0, share=self._loff)
        self.WB1 = nx("WB1", [128, 2, ND, 128], BF16, 4 * KBy)
        self.WB2 = nx("WB2", [128, 2, ND, 128], BF16, 4 * KBy)
        self.ST6 = nx("ST6", [128, NT, 12], F32, 768)
        st6o = self._loff
        self.STM = nx("STM", [128, 2, NT], F32, 128)
        self.GA = nx("GA", [128, NT, E], F32, 512)
        go = self._loff
        self.GB = nx("GB", [128, NT, E], F32, 512)
        self.GC = nx("GC", [128, NT, E], F32, 512)
        self.GSM = nx("GSM", [128, 4, NT], F32, 256)
        self.GATET = nx("GATET", [128, ND, 16], BF16, 256)
        self.RLB = nx("RLB", [128, 512], BF16, 0, share=go)
        self.BQ = nx("BQ", [128, 8], F32, 32)
        self.BK = nx("BK", [128, 8], F32, 32)
        self.BV = nx("BV", [128, 8], F32, 32)
        self.FBQ = nx("FBQ", [128, 8], F32, 32)
        self.FBK = nx("FBK", [128, 8], F32, 32)
        self.FBV = nx("FBV", [128, 8], F32, 32)
        self.B1B = nx("B1B", [128, E * NHT], F32, 512)
        self.B2B = nx("B2B", [128, E * NHT], F32, 512)
        self.G1 = nx("G1", [128, D], BF16, 2 * KBy)
        self.G2 = nx("G2", [128, D], BF16, 2 * KBy)
        self.ARENA = nx("ARENA", [128, 6144], BF16, 12 * KBy)
        ao_ = self._loff
        self.WB3 = nx("WB3", [128, 2, NHT, 128], BF16, 0, share=ao_)
        self.CO_IDENTF = nx("CO_IDENTF", [128, 128], F32, 0, share=st6o)
        self.CO_PERM = nx("CO_PERM", [128, 128], BF16, 256)
        self.CO_ONES1 = nx("CO_ONES1", [1, 128], BF16, 256)
        self.CO_ONES1F = nx("CO_ONES1F", [1, 128], F32, 512)
        self.CO_ONESC = nx("CO_ONESC", [128, 1], BF16, 32)
        self.CO_RSGN = nx("CO_RSGN", [128, 1], F32, 64)
        self.CPOOL = nx("CPOOL", [128, 8], F32, 32)
        assert o[0] + self._sbuf_BASE - 16 * KBy <= int(207.75 * KBy), f"sbuf over: {o[0]/1024:.1f}K"
        self.sbuf_top = o[0]

    def finalize(self):
        with self.nc.Block() as block:
            for eng in ENGS:
                getattr(block, eng)(self._body(self.q[eng]))
        self.es.close()
        return self.nc

    @staticmethod
    def _body(fns):
        def body(engine):
            for f in fns:
                f(engine)
        return body

    def prep_const_loads(self):
        cv = self.cin
        vv = self.vins
        pairs = [
            (self.CO_PERM[:, :], cv["PERM"]), (self.CO_ONES1[:, :], cv["ONES1"]),
            (self.CO_ONES1F[:, :], cv["ONES1F"]), (self.CO_ONESC[:, :], cv["ONESC"]),
            (self.CO_RSGN[:, :], cv["RSGN"]), (self.CO_IDENTF[:, :], cv["IDENTF"]),
            (self.BQ[:, :], vv["bq_pp"]), (self.BK[:, :], vv["bk_pp"]),
            (self.BV[:, :], vv["bv_pp"]),
            (self.FBQ[:, :], vv["fbq_pp"]), (self.FBK[:, :], vv["fbk_pp"]),
            (self.FBV[:, :], vv["fbv_pp"]),
            (self.B1B[:, :], vv["b1_pp"]), (self.B2B[:, :], vv["b2_pp"]),
            (self.G1[:, :], vv["g1_rep"]), (self.G2[:, :], vv["g2_rep"]),
        ]
        for d, s in pairs:
            self.dma("qg", d, s.ap())
        return self.cur("qg")

    def load_arena(self, kind):
        AR = self.ARENA
        W = 6144
        if kind == "a":
            self.dma("qg", A(AR, 0, [[W, 128], [1, 2560]]), self.cin["DSTRIP"].ap())
            v = self.dma("qg", A(AR, 2560, [[W, 128], [1, 1024]]), self.cin["CSTR"].ap())
        else:
            self.dma("qg", A(AR, 0, [[W, 128], [1, 2048]]), self.cin["COSF"].ap())
            v = self.dma("qg", A(AR, 2048, [[W, 128], [1, 2048]]), self.cin["SINF"].ap())
            row = self.vins["fob_row"] if kind == "inner" else self.vins["bo_row"]
            if kind == "b":
                self.dma("qg", A(AR, 4096, [[W, 128], [1, 1024]]), self.cin["WSTR"].ap())
            orow_c = 4096 if kind == "inner" else 5120
            v = self.dma("qg", A(AR, orow_c, [[W, 128], [1, D]])[0:1, :], row.ap())
        return v

    def prep_weights(self):
        def prep_OD(src_off, srct, nch, ndt, slab=None, row=None):
            Dp = ndt * 128
            stage = A(self.PSTG, 0, [[8 * 3072, 128], [Dp, nch], [1, Dp]])
            vl = self.dma("qg", stage, A(srct, src_off, [[Dp, 128], [128 * Dp, nch], [1, Dp]]))
            self.wait("sync", "qg", vl)
            tin = A(self.PSTG, 0, [[8 * 3072, 128], [128, nch * ndt], [1, 128]])
            tout = A(self.TSTG, 0, [[192 * 128, 128], [128, nch * ndt], [1, 128]])
            self.dma("qw", tout, tin, transpose=True)
            last = 0
            if slab is not None:
                dram, base = slab
                outap = A(dram, base, [[ndt * 128, 128], [128 * ndt * 128, nch],
                                       [1, ndt * 128]])
                inap = A(self.TSTG, 0, [[192 * 128, 128], [ndt * 128, nch],
                                        [1, ndt * 128]])
                last = self.dma("qw", outap, inap)
            if row is not None:
                dram, ncols, col0 = row
                for dtt in range(ndt):
                    outap = A(dram, col0 + dtt * 128 * ncols,
                              [[ncols, 128], [128, nch], [1, 128]])
                    inap = A(self.TSTG, dtt * 128,
                             [[192 * 128, 128], [ndt * 128, nch], [1, 128]])
                    last = self.dma("qw", outap, inap)
            self.wait("gpsimd", "qw", last)
            return last

        sl, ro, w = self.sc_slab, self.sc_row, self.w
        self._prep_ao()
        prep_OD(0, w["fqkv"], 8, ND, slab=(sl["fq"], 0))
        prep_OD(8 * 128 * D, w["fqkv"], 8, ND, slab=(sl["fk"], 0))
        prep_OD(16 * 128 * D, w["fqkv"], 8, ND, slab=(sl["fv"], 0))
        prep_OD(0, w["fout"], 8, ND, row=(ro["fo"], D, 0))
        if self.upto >= 3:
            prep_OD(0, w["bWq"], 8, ND, slab=(sl["bq"], 0))
            prep_OD(0, w["bWk"], 8, ND, slab=(sl["bk"], 0))
            prep_OD(0, w["bWv"], 8, ND, slab=(sl["bv"], 0))
            prep_OD(0, w["bWo"], 8, ND, row=(ro["bo"], D, 0))
        self.wait("vector", "qw", self.cur("qw"))
        vz = self.bump("dv")
        self.em("vector", lambda e: e.memset(
            A(self.PSTG, 0, [[8 * 3072, 16], [1, D]]), 0.0).then_inc(self.SM["dv"], 1))
        self.wait("gpsimd", "dv", vz)
        vg = self.dma("qg", A(self.PSTG, 0, [[8 * 3072, 8], [1, D]]),
                      A(self.w["fgate"], 0, [[D, 8], [1, D]]))
        self.wait("sync", "qg", vg)
        tin = A(self.PSTG, 0, [[8 * 3072, 16], [128, ND], [1, 128]])
        tout = A(self.TSTG, 0, [[192 * 128, 128], [16, ND], [1, 16]])
        self.dma("qw", tout, tin, transpose=True)
        vgs = self.dma("qw", A(ro["gate"], 0, [[16, 128], [128 * 16, ND], [1, 16]]),
                       A(self.TSTG, 0, [[192 * 128, 128], [16, ND], [1, 16]]))
        self.wait("gpsimd", "qw", vgs)
        if self.upto >= 2:
            for e in range(E):
                prep_OD(e * HH * D, w["fw1"], 16, ND, slab=(sl["w1"], e * NHT * SL))
                prep_OD(e * HH * D, w["fw2"], 16, ND, slab=(sl["w2"], e * NHT * SL))
                prep_OD(e * D * HH, w["fw3"], 8, NHT, slab=(sl["w3"], e * ND * 128 * NHT * 128))
        self.prep_done = (self.cur("qw"), self.cur("qg"))

    def _prep_ao(self):
        vl = self.dma("qg", self.PSTG[:, :, :],
                      A(self.w["aW"], 0, [[3072, 128], [128 * 3072, 8], [1, 3072]]))
        self.wait("sync", "qg", vl)
        va = 0
        for name, ocb in (("aq", 0), ("ak", 8), ("av", 16)):
            for oc in range(8):
                outap = A(self.sc_slab[name], oc * SL,
                          [[ND * 128, 128], [128, ND], [1, 128]])
                inap = A(self.PSTG, (ocb + oc) * 128,
                         [[8 * 3072, 128], [3072, ND], [1, 128]])
                va = self.dma("qw", outap, inap)
        self.wait("gpsimd", "qw", va)
        vp = self.dma("qg", A(self.PSTG, 0, [[8 * 3072, 128], [D, ND], [1, D]]),
                      A(self.w["aP"], 0, [[D, 128], [128 * D, ND], [1, D]]))
        self.wait("sync", "qg", vp)
        vs = self.dma("qw", A(self.sc_row["ao"], 0, [[D, 128], [128 * D, ND], [1, D]]),
                      A(self.PSTG, 0, [[8 * 3072, 128], [D, ND], [1, D]]))
        self.wait("gpsimd", "qw", vs)

    def load_x(self, b):
        return self.dma("qg", self.XR[:, :, :],
                        A(self.xin, b * S * D, [[D, 128], [128 * D, NT], [1, D]]))

    def store_x(self, b):
        return self.dma("qo", A(self.xout, b * S * D, [[D, 128], [128 * D, NT], [1, D]]),
                        self.XR[:, :, :])

    def spill_x(self):
        return self.dma("qo", A(self.xspill, 0, [[D, 128], [128 * D, NT], [1, D]]),
                        self.XR[:, :, :])

    def reload_x(self):
        return self.dma("qg", self.XR[:, :, :],
                        A(self.xspill, 0, [[D, 128], [128 * D, NT], [1, D]]))

    def norm_pass(self, src, mode, dep=None):
        if dep is not None:
            self.wait("vector", dep[0], dep[1])
            self.wait("scalar", dep[0], dep[1])
        for tt in range(NT):
            self.em("vector", lambda e, t=tt: e.bn_stats(self.ST6[:, t, 0:6],
                                                         src[:, t, 0:512]))
            self.em("vector", lambda e, t=tt: e.bn_stats(self.ST6[:, t, 6:12],
                                                         src[:, t, 512:1024]))
        self.xfence("vector")
        for tt in range(NT):
            self.em("vector", lambda e, t=tt: e.bn_aggr(
                self.GA[:, t, 0:2], A(self.ST6, t * 12, [[NT * 12, 128], [1, 6]])))
            self.em("vector", lambda e, t=tt: e.bn_aggr(
                self.GA[:, t, 2:4], A(self.ST6, t * 12 + 6, [[NT * 12, 128], [1, 6]])))
        self.xfence("vector")
        m1v = A(self.GA, 0, [[NT * E, 128], [E, NT]])
        v1v = A(self.GA, 1, [[NT * E, 128], [E, NT]])
        m2v = A(self.GA, 2, [[NT * E, 128], [E, NT]])
        v2v = A(self.GA, 3, [[NT * E, 128], [E, NT]])
        tmpd = A(self.GA, 4, [[NT * E, 128], [E, NT]])
        smean = self.STM[:, 0, :]
        svar = self.STM[:, 1, :]

        def VV(f):
            self.em("vector", f)
            self.xfence("vector")

        VV(lambda e: e.tensor_add(smean, m1v, m2v))
        VV(lambda e: e.tensor_scalar(smean, smean, 0.5, None, ALU.mult))
        VV(lambda e: e.tensor_sub(tmpd, m1v, m2v))
        VV(lambda e: e.tensor_mul(tmpd, tmpd, tmpd))
        VV(lambda e: e.tensor_scalar(tmpd, tmpd, 0.25, None, ALU.mult))
        VV(lambda e: e.tensor_add(svar, v1v, v2v))
        VV(lambda e: e.tensor_scalar(svar, svar, 0.5, None, ALU.mult))
        VV(lambda e: e.tensor_add(svar, svar, tmpd))
        vda = self.xfence("vector")
        meanv = self.STM[:, 0, :]
        varv = self.STM[:, 1, :]
        g0 = self.GSM[:, 0, :]
        g1 = self.GSM[:, 1, :]
        g2 = self.GSM[:, 2, :]
        ln = mode.startswith("ln")
        if ln:
            self.wait("scalar", "dv", vda)
            self.em("scalar", lambda e: e.activation(g1, varv, AF.Sqrt, bias=1e-5))
            vsc = self.scdrain()
            self.wait("vector", "sc", vsc)
            self.em("vector", lambda e: e.reciprocal(g0, g1))
            self.xfence("vector")
            self.em("vector", lambda e: e.tensor_scalar(g1, meanv, -1.0, None, ALU.mult))
            self.xfence("vector")
            self.em("vector", lambda e: e.tensor_mul(g2, g1, g0))
            vready = self.xfence("vector")
            self.wait("scalar", "dv", vready)
        else:
            self.em("vector", lambda e: e.tensor_mul(g1, meanv, meanv))
            self.xfence("vector")
            self.em("vector", lambda e: e.tensor_add(g1, g1, varv))
            vready = self.xfence("vector")
            self.wait("scalar", "dv", vready)
            self.em("scalar", lambda e: e.activation(g1, g1, AF.Sqrt, bias=1e-6))
            vsc = self.scdrain()
            self.wait("vector", "sc", vsc)
            self.em("vector", lambda e: e.reciprocal(g0, g1))
            vready = self.xfence("vector")
            self.wait("scalar", "dv", vready)
        gamma = {"rms1": self.G1, "rms2": self.G2}.get(mode)
        if mode == "ln_tr":
            for tt in range(NT):
                self.em("scalar", lambda e, t=tt: e.activation(
                    self.TR[:, t, :], src[:, t, :], AF.Identity,
                    scale=self.GSM[:, 0, t:t + 1], bias=self.GSM[:, 2, t:t + 1]))
            v = self.xfence("scalar")
            return ("sc", v)
        last_tp = [0, 0]
        for tt in range(NT):
            tb = tt % 2
            if tt >= 2:
                self.wait("scalar", "qw", last_tp[tb])
            if ln:
                self.em("scalar", lambda e, t=tt, b=tb: e.activation(
                    self.XTOK[:, b, :], src[:, t, :], AF.Identity,
                    scale=self.GSM[:, 0, t:t + 1], bias=self.GSM[:, 2, t:t + 1]))
            else:
                self.em("scalar", lambda e, t=tt, b=tb: e.activation(
                    self.XTOK[:, b, :], src[:, t, :], AF.Identity,
                    scale=self.GSM[:, 0, t:t + 1]))
            vsc = self.scdrain()
            if gamma is not None:
                self.wait("vector", "sc", vsc)
                self.em("vector", lambda e, b=tb: e.tensor_mul(
                    self.XTOK[:, b, :], self.XTOK[:, b, :], gamma[:, :])
                    .then_inc(self.SM["dv"], 1))
                vg = self.bump("dv")
                self.wait("sync", "dv", vg)
            else:
                self.wait("sync", "sc", vsc)
            tout = A(self.XNT, 128 * tt, [[ND * S, 128], [S, ND], [1, 128]])
            tin = A(self.XTOK, tb * D, [[2 * D, 128], [128, ND], [1, 128]])
            last_tp[tb] = self.dma("qw", tout, tin, transpose=True)
        return ("qw", max(last_tp))

    def attn(self, kind, xnt_ready, sub="full"):
        cfg = {
            "a": dict(dst=self.XR, V=self.V_B, OT=self.OT_B,
                      slabQ=self.sc_slab["aq"], slabK=self.sc_slab["ak"],
                      slabV=self.sc_slab["av"], rowO=self.sc_row["ao"],
                      bq=None, bk=None, bv=None, orow=None, rope=False),
            "inner": dict(dst=self.TR, V=self.V_A, OT=self.OT_A,
                          slabQ=self.sc_slab["fq"], slabK=self.sc_slab["fk"],
                          slabV=self.sc_slab["fv"], rowO=self.sc_row["fo"],
                          bq=self.FBQ, bk=self.FBK, bv=self.FBV,
                          orow=4096, rope=True),
            "b": dict(dst=self.XR, V=self.V_B, OT=self.OT_B,
                      slabQ=self.sc_slab["bq"], slabK=self.sc_slab["bk"],
                      slabV=self.sc_slab["bv"], rowO=self.sc_row["bo"],
                      bq=self.BQ, bk=self.BK, bv=self.BV,
                      orow=5120, rope=True),
        }[kind]
        Vb, OTb, dst = cfg["V"], cfg["OT"], cfg["dst"]
        expc = 64.0 if kind == "inner" else 32.0
        psS, psT = self.psS, self.psT
        af = getattr(self, "_arena_free", ("dv", 0))
        self.wait("gpsimd", af[0], af[1])
        arena_v = self.load_arena(kind)
        for eng in ("tensor", "scalar", "vector"):
            self.wait(eng, xnt_ready[0], xnt_ready[1])
        vload = [0, 0]
        vmm_pe = [0, 0]
        evac_sc = [0, 0]
        tdma = [0, 0]
        for oc in range(8):
            wb = oc % 2
            if oc >= 2:
                self.wait("gpsimd", "pe", vmm_pe[wb])
            vload[wb] = self.dma("qg", self.WB1[:, wb, :, :],
                                 A(cfg["slabV"], oc * SL, [[ND * 128, 128], [128, ND], [1, 128]]))
            ps = [psS, psT][wb]
            self.wait("tensor", "qg", vload[wb])
            if oc >= 2 and sub not in ("vmm",):
                self.wait("tensor", "sc", evac_sc[wb])
            for dt in range(ND):
                for qn in range(4):
                    fin = (dt == ND - 1 and qn == 3)
                    self.em("tensor", lambda e, w=wb, d=dt, q=qn, p=ps, f=fin:
                            (e.matmul(p[:, q * 512:(q + 1) * 512], self.WB1[:, w, d, :],
                                      self.XNT[:, d, q * 512:(q + 1) * 512],
                                      start=(d == 0), stop=(d == ND - 1))
                             .then_inc(self.SM["pe"], 1)) if f else
                            e.matmul(p[:, q * 512:(q + 1) * 512], self.WB1[:, w, d, :],
                                     self.XNT[:, d, q * 512:(q + 1) * 512],
                                     start=(d == 0), stop=(d == ND - 1)))
            vmm_pe[wb] = self.bump("pe")
            if sub == "vmm":
                continue
            self.wait("scalar", "pe", vmm_pe[wb])
            if oc >= 1:
                self.wait("scalar", "qw", tdma[1 - wb])
            bv = cfg["bv"]
            for hf in range(2):
                if bv is not None:
                    self.em("scalar", lambda e, p=ps, o=oc, b=bv, h=hf: e.activation(
                        self.QT[:, h * 1024:(h + 1) * 1024], p[:, h * 1024:(h + 1) * 1024],
                        AF.Identity, bias=b[:, o:o + 1]))
                else:
                    self.em("scalar", lambda e, p=ps, h=hf: e.activation(
                        self.QT[:, h * 1024:(h + 1) * 1024], p[:, h * 1024:(h + 1) * 1024],
                        AF.Identity))
            evac_sc[wb] = self.scdrain()
            if sub == "vev":
                continue
            self.wait("sync", "sc", evac_sc[wb])
            tout = A(Vb, 128 * oc, [[NT * D, 128], [D, NT], [1, 128]])
            tin = A(self.QT, 0, [[S, 128], [128, NT], [1, 128]])
            tdma[wb] = self.dma("qw", tout, tin, transpose=True)
        if sub == "vmm":
            return ("pe", max(vmm_pe))
        if sub == "vev":
            return ("sc", max(evac_sc))
        v_ready = max(tdma)
        if sub == "v":
            return ("qw", v_ready)
        self.wait("tensor", "qg", arena_v)
        self.wait("vector", "qg", arena_v)
        last_qk_pe = max(vmm_pe)
        for pr in range(NPAIR):
            self.wait("gpsimd", "pe", last_qk_pe)
            vq = self.dma("qg", self.WB1[:, 0, :, :],
                          A(cfg["slabQ"], pr * SL, [[ND * 128, 128], [128, ND], [1, 128]]))
            vk = self.dma("qg", self.WB2[:, 0, :, :],
                          A(cfg["slabK"], pr * SL, [[ND * 128, 128], [128, ND], [1, 128]]))
            self.wait("tensor", "qg", vk)
            if pr == 0:
                self.wait("tensor", "qw", v_ready)
                self.wait("tensor", "sc", max(evac_sc))
                self.wait("scalar", "qw", v_ready)
            qk_evac = []
            for which, wbuf, ps, bias in ((self.QT, self.WB1, psS, cfg["bq"]),
                                          (self.KT, self.WB2, psT, cfg["bk"])):
                for th in range(2):
                    for dt in range(ND):
                        for qn in range(2):
                            fin = (dt == ND - 1 and qn == 1)
                            self.em("tensor", lambda e, w=wbuf, d=dt, t=th, q=qn, p=ps, f=fin:
                                    (e.matmul(p[:, t * 1024 + q * 512:t * 1024 + (q + 1) * 512],
                                              w[:, 0, d, :],
                                              self.XNT[:, d, t * 1024 + q * 512:t * 1024 + (q + 1) * 512],
                                              start=(d == 0), stop=(d == ND - 1))
                                     .then_inc(self.SM["pe"], 1)) if f else
                                    e.matmul(p[:, t * 1024 + q * 512:t * 1024 + (q + 1) * 512],
                                             w[:, 0, d, :],
                                             self.XNT[:, d, t * 1024 + q * 512:t * 1024 + (q + 1) * 512],
                                             start=(d == 0), stop=(d == ND - 1)))
                    vpe = self.bump("pe")
                    self.wait("scalar", "pe", vpe)
                    if bias is not None:
                        self.em("scalar", lambda e, o=which, t=th, p=ps, bb=bias, r=pr:
                                e.activation(o[:, t * 1024:(t + 1) * 1024],
                                             p[:, t * 1024:(t + 1) * 1024],
                                             AF.Identity, bias=bb[:, r:r + 1]))
                    else:
                        self.em("scalar", lambda e, o=which, t=th, p=ps:
                                e.activation(o[:, t * 1024:(t + 1) * 1024],
                                             p[:, t * 1024:(t + 1) * 1024], AF.Identity))
                    qk_evac.append(self.scdrain())
            last_qk_pe = self.cur("pe")
            self.wait("tensor", "sc", qk_evac[-1])
            if sub == "qkv":
                continue
            if cfg["rope"]:
                rope_dvs = [0, 0]
                rope_dv = 0
                for ii, (buf, th) in enumerate([(self.QT, 0), (self.QT, 1),
                                                (self.KT, 0), (self.KT, 1)]):
                    ps = [psS, psT][ii % 2]
                    if ii >= 2:
                        self.wait("tensor", "dv", rope_dvs[ii % 2])
                    for qn in range(2):
                        self.em("tensor", lambda e, b=buf, t=th, q=qn, p=ps, f=(qn == 1):
                                (e.matmul(p[:, q * 512:(q + 1) * 512], self.CO_PERM[:, :],
                                          b[:, t * 1024 + q * 512:t * 1024 + (q + 1) * 512],
                                          start=True, stop=True)
                                 .then_inc(self.SM["pe"], 1)) if f else
                                e.matmul(p[:, q * 512:(q + 1) * 512], self.CO_PERM[:, :],
                                         b[:, t * 1024 + q * 512:t * 1024 + (q + 1) * 512],
                                         start=True, stop=True))
                    vpe = self.bump("pe")
                    self.wait("vector", "pe", vpe)
                    self.em("vector", lambda e, p=ps, t=th: e.scalar_tensor_tensor(
                        self.PB[:, 2, :], p[:, 0:1024], self.CO_RSGN[:, 0:1],
                        A(self.ARENA, 2048 + t * 1024, [[6144, 128], [1, 1024]]),
                        ALU.mult, ALU.mult))
                    self.em("vector", lambda e, b=buf, t=th: e.tensor_mul(
                        self.PB[:, 0, :], b[:, t * 1024:(t + 1) * 1024],
                        A(self.ARENA, t * 1024, [[6144, 128], [1, 1024]])))
                    self.em("vector", lambda e, b=buf, t=th: e.tensor_add(
                        b[:, t * 1024:(t + 1) * 1024], self.PB[:, 0, :], self.PB[:, 2, :])
                        .then_inc(self.SM["dv"], 1))
                    rope_dv = self.bump("dv")
                    rope_dvs[ii % 2] = rope_dv
                self.wait("tensor", "dv", rope_dv)
                self.wait("scalar", "dv", rope_dv)
            if sub == "rope":
                continue
            for qq in range(4):
                if kind == "a":
                    ks = list(range(0, 4 * qq + 4))
                elif kind == "inner":
                    ks = list(range(16))
                else:
                    ks = list(range(max(0, 4 * qq - 1), 4 * qq + 4))
                exp_sc = [0, 0]
                pv_pe = [0, 0]
                dec_dv = [0, 0]
                nks = len(ks)
                for idx, j in enumerate(ks):
                    half = idx % 2
                    soff = half * 1024
                    if idx >= 2:
                        self.wait("tensor", "sc", exp_sc[half])
                    for (a0, tp) in ((0, (0, 0)), (64, (64, 0))):
                        self.em("tensor", lambda e, a=a0, t=tp, jj=j, q=qq, o=soff, f=(a0 == 64):
                                (e.matmul(self.psS[:, o + a * 8:o + a * 8 + 512],
                                          self.KT[a:a + 64, 128 * jj:128 * jj + 128],
                                          self.QT[a:a + 64, 512 * q:512 * q + 512],
                                          start=True, stop=True, tile_position=t)
                                 .then_inc(self.SM["pe"], 1)) if f else
                                e.matmul(self.psS[:, o + a * 8:o + a * 8 + 512],
                                         self.KT[a:a + 64, 128 * jj:128 * jj + 128],
                                         self.QT[a:a + 64, 512 * q:512 * q + 512],
                                         start=True, stop=True, tile_position=t))
                    vpe = self.bump("pe")
                    self.wait("scalar", "pe", vpe)
                    if idx >= 2:
                        self.wait("scalar", "pe", pv_pe[half])
                        if kind == "a":
                            self.wait("scalar", "dv", dec_dv[half])
                    self.em("scalar", lambda e, h=half, o=soff: e.activation(
                        self.PB[:, h, :], self.psS[:, o:o + 1024], AF.Exp,
                        scale=np.float32(EXPS), bias=np.float32(-expc)))
                    exp_sc[half] = self.scdrain()
                    need_mask = (kind == "b") or (kind == "a" and j >= 4 * qq)
                    moff = 512 + 512 * qq - 128 * j
                    if need_mask:
                        base = 2560 if kind == "a" else 4096 - 128
                        self.wait("vector", "sc", exp_sc[half])
                        self.em("vector", lambda e, h=half, m=moff, bs=base: e.tensor_mul(
                            self.PB[:, h, :], self.PB[:, h, :],
                            A(self.ARENA, bs + m, [[6144, 128], [0, 2], [1, 512]]))
                            .then_inc(self.SM["dv"], 1))
                        vm = self.bump("dv")
                        self.wait("tensor", "dv", vm)
                    else:
                        self.wait("tensor", "sc", exp_sc[half])
                    for s2 in range(2):
                        self.em("tensor", lambda e, h=half, s3=s2, i=idx, f=(s2 == 1 and kind == "a"):
                                (e.matmul(
                                    self.psT[0:1, 1024 + 512 * s3:1536 + 512 * s3],
                                    self.CO_ONESC[:, 0:1], self.PB[:, h, 512 * s3:512 * s3 + 512],
                                    start=(i == 0), stop=(i == nks - 1))
                                 .then_inc(self.SM["pe"], 1)) if f else
                                e.matmul(
                                    self.psT[0:1, 1024 + 512 * s3:1536 + 512 * s3],
                                    self.CO_ONESC[:, 0:1], self.PB[:, h, 512 * s3:512 * s3 + 512],
                                    start=(i == 0), stop=(i == nks - 1)))
                    pv_src = half
                    if kind == "a":
                        vl = self.bump("pe")
                        self.wait("vector", "pe", vl)
                        self.em("vector", lambda e, h=half, m=moff: e.tensor_mul(
                            self.PB[:, 2, :], self.PB[:, h, :],
                            A(self.ARENA, m, [[6144, 128], [0, 2], [1, 512]]))
                            .then_inc(self.SM["dv"], 1))
                        dec_dv[half] = self.bump("dv")
                        self.wait("tensor", "dv", dec_dv[half])
                        pv_src = 2
                    for cg in (0, 64):
                        self.em("tensor", lambda e, c=cg, h=pv_src, jj=j, i=idx, r=pr, f=(cg == 64):
                                (e.matmul(self.psT[c:c + 64, 0:512],
                                          Vb[:, jj, 128 * r + c:128 * r + c + 64],
                                          self.PB[:, h, c * 8:c * 8 + 512],
                                          start=(i == 0), stop=(i == nks - 1),
                                          tile_position=(0, c))
                                 .then_inc(self.SM["pe"], 1)) if f else
                                e.matmul(self.psT[c:c + 64, 0:512],
                                         Vb[:, jj, 128 * r + c:128 * r + c + 64],
                                         self.PB[:, h, c * 8:c * 8 + 512],
                                         start=(i == 0), stop=(i == nks - 1),
                                         tile_position=(0, c)))
                    pv_pe[half] = self.bump("pe")
                self.wait("vector", "pe", self.cur("pe"))
                self.em("vector", lambda e: e.reciprocal(self.RR[0:1, :],
                                                         self.psT[0:1, 1024:2048])
                        .then_inc(self.SM["dv"], 1))
                self.bump("dv")
                vr = self.xfence("vector")
                self.wait("tensor", "dv", vr)
                for s2 in range(2):
                    self.em("tensor", lambda e, s3=s2, f=(s2 == 1): (e.matmul(
                        self.psT[64 * s3:64 * s3 + 64, 512:1024],
                        self.CO_ONES1F[0:1, 0:64], self.RR[0:1, 512 * s3:512 * s3 + 512],
                        start=True, stop=True).then_inc(self.SM["pe"], 1)) if f else e.matmul(
                        self.psT[64 * s3:64 * s3 + 64, 512:1024],
                        self.CO_ONES1F[0:1, 0:64], self.RR[0:1, 512 * s3:512 * s3 + 512],
                        start=True, stop=True))
                vpe = self.bump("pe")
                self.wait("scalar", "pe", vpe)
                self.em("scalar", lambda e: e.activation(self.RLB[:, :], self.psT[:, 512:1024],
                                                         AF.Identity))
                vsc = self.scdrain()
                self.wait("vector", "sc", vsc)
                self.em("vector", lambda e, r=pr, q=qq, o=OTb: e.tensor_mul(
                    o[:, r, 512 * q:512 * q + 512], self.psT[:, 0:512], self.RLB[:, :])
                    .then_inc(self.SM["dv"], 1))
                votq = self.bump("dv")
                self.wait("tensor", "dv", votq)
        ot_done = self.cur("dv")
        if sub in ("qkv", "rope"):
            return ("dv", self.cur("dv"))
        if sub == "sloop":
            return ("dv", ot_done)
        self.wait("gpsimd", "dv", ot_done)
        opl = 0
        for pr2 in range(NPAIR):
            opl = self.dma("qg", A(self.OPW, pr2 * 1024, [[8 * 1024, 128], [1, 1024]]),
                           A(cfg["rowO"], pr2 * 128 * D, [[D, 128], [1, D]]))
        self.wait("tensor", "qg", opl)
        ev_dv = [0, 0]
        for tc in range(NT):
            half = tc % 2
            ps = [psS, psT][half]
            if tc >= 2:
                self.wait("tensor", "dv", ev_dv[half])
            last_is_mm = cfg["orow"] is None
            for pr2 in range(NPAIR):
                for qn in range(2):
                    fin = last_is_mm and pr2 == NPAIR - 1 and qn == 1
                    self.em("tensor", lambda e, r=pr2, t=tc, q=qn, p=ps, o=OTb, f=fin:
                            (e.matmul(p[:, q * 512:(q + 1) * 512],
                                      o[:, r, 128 * t:128 * t + 128],
                                      A(self.OPW, r * 1024 + q * 512, [[8 * 1024, 128], [1, 512]]),
                                      start=(r == 0),
                                      stop=(r == NPAIR - 1 and last_is_mm))
                             .then_inc(self.SM["pe"], 1)) if f else
                            e.matmul(p[:, q * 512:(q + 1) * 512],
                                     o[:, r, 128 * t:128 * t + 128],
                                     A(self.OPW, r * 1024 + q * 512, [[8 * 1024, 128], [1, 512]]),
                                     start=(r == 0),
                                     stop=(r == NPAIR - 1 and last_is_mm)))
            if cfg["orow"] is not None:
                ro = cfg["orow"]
                for qn in range(2):
                    self.em("tensor", lambda e, q=qn, p=ps, r0=ro, f=(qn == 1): (e.matmul(
                        p[:, q * 512:(q + 1) * 512], self.CO_ONES1[0:1, :],
                        A(self.ARENA, r0 + q * 512, [[6144, 1], [1, 512]]),
                        start=False, stop=(q == 1)).then_inc(self.SM["pe"], 1)) if f else e.matmul(
                        p[:, q * 512:(q + 1) * 512], self.CO_ONES1[0:1, :],
                        A(self.ARENA, r0 + q * 512, [[6144, 1], [1, 512]]),
                        start=False, stop=(q == 1)))
            vpe = self.bump("pe")
            self.wait("vector", "pe", vpe)
            self.em("vector", lambda e, t=tc, p=ps, d=dst: e.tensor_add(
                d[:, t, :], d[:, t, :], p[:, 0:1024]).then_inc(self.SM["dv"], 1))
            ev_dv[half] = self.bump("dv")
        self._arena_free = ("dv", self.cur("dv"))
        return ("dv", self.cur("dv"))
    def moe(self, xnt_ready, x_dep):
        psS, psT = self.psS, self.psT
        gl = self.dma("qg", self.GATET[:, :, :],
                      A(self.sc_row["gate"], 0, [[16, 128], [128 * 16, ND], [1, 16]]))
        self.wait("gpsimd", xnt_ready[0], xnt_ready[1])
        idv = self.dma("qg", self.CO_IDENTF[:, :], self.cin["IDENTF"].ap())
        for eng in ("tensor", "vector"):
            self.wait(eng, xnt_ready[0], xnt_ready[1])
        self.wait("tensor", "qg", idv)
        self.wait("tensor", "qg", gl)
        cp_dv = [0, 0]
        for tt in range(NT):
            pz = tt % 2
            col = 512 * pz
            if tt >= 2:
                self.wait("tensor", "dv", cp_dv[pz])
            for dt in range(ND):
                self.em("tensor", lambda e, d=dt, t=tt, c=col, f=(dt == ND - 1): (e.matmul(
                    psT[:, c:c + 8], self.XNT[:, d, 128 * t:128 * t + 128],
                    self.GATET[:, d, 0:8], start=(d == 0), stop=(d == ND - 1))
                    .then_inc(self.SM["pe"], 1)) if f else e.matmul(
                    psT[:, c:c + 8], self.XNT[:, d, 128 * t:128 * t + 128],
                    self.GATET[:, d, 0:8], start=(d == 0), stop=(d == ND - 1)))
            vpe = self.bump("pe")
            self.wait("vector", "pe", vpe)
            self.em("vector", lambda e, t=tt, c=col: e.tensor_copy(
                self.GA[:, t, :], psT[:, c:c + 8]).then_inc(self.SM["dv"], 1))
            cp_dv[pz] = self.bump("dv")
        self.xfence("vector")
        m1 = self.GSM[:, 0, :]
        m2 = self.GSM[:, 1, :]
        w1g = self.GSM[:, 2, :]
        w2g = self.GSM[:, 3, :]
        m1b = A(self.GSM, 0 * NT, [[NT * 4, 128], [1, NT], [0, E]])
        m2b = A(self.GSM, 1 * NT, [[NT * 4, 128], [1, NT], [0, E]])
        w1b = A(self.GSM, 2 * NT, [[NT * 4, 128], [1, NT], [0, E]])
        w2b = A(self.GSM, 3 * NT, [[NT * 4, 128], [1, NT], [0, E]])
        ga, gb, gc = self.GA, self.GB, self.GC

        def V(f):
            self.em("vector", f)
            self.xfence("vector")

        V(lambda e: e.reduce_max(m1, ga[:, :, :], AX.X))
        V(lambda e: e.tensor_tensor(gb[:, :, :], ga[:, :, :], m1b, ALU.is_equal))
        V(lambda e: e.scalar_tensor_tensor(gc[:, :, :], gb[:, :, :], -1e30, ga[:, :, :],
                                           ALU.mult, ALU.add))
        V(lambda e: e.reduce_max(m2, gc[:, :, :], AX.X))
        V(lambda e: e.tensor_tensor(w1g, m1, m2, ALU.subtract))
        vdv = self.cur("dv")
        self.wait("scalar", "dv", vdv)
        self.em("scalar", lambda e: e.activation(w2g, w1g, AF.Exp, scale=-1.0)
                .then_inc(self.SM["sc"], 1))
        self.bump("sc")
        vsc = self.xfence("scalar")
        self.wait("vector", "sc", vsc)
        V(lambda e: e.tensor_scalar(w2g, w2g, 1.0, None, ALU.add))
        V(lambda e: e.reciprocal(w1g, w2g))
        V(lambda e: e.tensor_scalar(w2g, w1g, -1.0, 1.0, ALU.mult, ALU.add))
        V(lambda e: e.tensor_tensor(ga[:, :, :], gc[:, :, :], m2b, ALU.is_equal))
        V(lambda e: e.tensor_mul(gb[:, :, :], gb[:, :, :], w1b))
        V(lambda e: e.tensor_mul(ga[:, :, :], ga[:, :, :], w2b))
        V(lambda e: e.tensor_add(gc[:, :, :], ga[:, :, :], gb[:, :, :]))
        comb_rdy = self.cur("dv")
        self.wait("vector", x_dep[0], x_dep[1])
        self.wait("tensor", "dv", comb_rdy)
        ring = [psS, psT]
        rel = [("dv", 0), ("dv", 0)]
        it = [0]

        def slot():
            z = it[0] % 2
            sem, v = rel[z]
            self.wait("tensor", sem, v)
            it[0] += 1
            return z, ring[z]

        wuse1 = [0, 0]
        wuse3 = [0, 0]
        s1_dv = 0
        vh_last = 0
        prev_p2_pe = 0
        ytok_dv = 0
        for e_ in range(E):
            self.wait("gpsimd", "pe", prev_p2_pe)
            b3v = self.dma("qg", self.B3ROW[0:1, :],
                           A(self.vins["b3_r"], e_ * D, [[D, 1], [1, D]]))
            z, ps = slot()
            crow_sc = [0, 0, 0, 0]
            for tt in range(NT):
                bank = tt % 4
                if tt >= 4:
                    self.wait("tensor", "dv", crow_sc[bank])
                self.em("tensor", lambda e, t=tt, bk=bank, p=ps, ee=e_: e.transpose(
                    p[0:1, 512 * bk:512 * bk + 128], self.GC[:, t, ee:ee + 1],
                    self.CO_IDENTF[:, :]).then_inc(self.SM["pe"], 1))
                vpe = self.bump("pe")
                self.wait("vector", "pe", vpe)
                self.em("vector", lambda e, t=tt, bk=bank, p=ps: e.tensor_copy(
                    self.CROW[0:1, 128 * t:128 * t + 128],
                    p[0:1, 512 * bk:512 * bk + 128])
                    .then_inc(self.SM["dv"], 1))
                crow_sc[bank] = self.bump("dv")
            self.wait("tensor", "dv", max(crow_sc))
            for qn in range(4):
                self.em("tensor", lambda e, q=qn, p=ps, f=(qn == 3): (e.matmul(
                    p[:, q * 512:(q + 1) * 512], self.CO_ONES1[0:1, :],
                    self.CROW[0:1, q * 512:(q + 1) * 512], start=True, stop=True)
                    .then_inc(self.SM["pe"], 1)) if f else e.matmul(
                    p[:, q * 512:(q + 1) * 512], self.CO_ONES1[0:1, :],
                    self.CROW[0:1, q * 512:(q + 1) * 512], start=True, stop=True))
            vpe = self.bump("pe")
            self.wait("scalar", "pe", vpe)
            self.wait("scalar", "dv", vh_last)
            for hf in range(2):
                self.em("scalar", lambda e, h=hf, p=ps: e.activation(
                    self.CMB[:, h * 1024:(h + 1) * 1024], p[:, h * 1024:(h + 1) * 1024],
                    AF.Identity))
            vcmb = self.scdrain()
            rel[z] = ("sc", vcmb)
            self.wait("vector", "sc", vcmb)
            for hc in range(NHT):
                wz = hc % 2
                if hc >= 2 or e_ > 0:
                    self.wait("gpsimd", "pe", wuse1[wz])
                v1 = self.dma("qg", self.WB1[:, wz, :, :],
                              A(self.sc_slab["w1"], (e_ * NHT + hc) * SL,
                                [[ND * 128, 128], [128, ND], [1, 128]]))
                v2 = self.dma("qg", self.WB2[:, wz, :, :],
                              A(self.sc_slab["w2"], (e_ * NHT + hc) * SL,
                                [[ND * 128, 128], [128, ND], [1, 128]]))
                self.wait("tensor", "qg", v2)
                for th in range(2):
                    z, ps = slot()
                    for dt in range(ND):
                        for qn in range(2):
                            self.em("tensor", lambda e, w=wz, d=dt, t=th, q=qn, p=ps:
                                    e.matmul(p[:, q * 512:(q + 1) * 512],
                                             self.WB1[:, w, d, :],
                                             self.XNT[:, d, t * 1024 + q * 512:t * 1024 + (q + 1) * 512],
                                             start=(d == 0), stop=(d == ND - 1)))
                    for dt in range(ND):
                        for qn in range(2):
                            fin = (dt == ND - 1 and qn == 1)
                            self.em("tensor", lambda e, w=wz, d=dt, t=th, q=qn, p=ps, f=fin:
                                    (e.matmul(p[:, 1024 + q * 512:1024 + (q + 1) * 512],
                                              self.WB2[:, w, d, :],
                                              self.XNT[:, d, t * 1024 + q * 512:t * 1024 + (q + 1) * 512],
                                              start=(d == 0), stop=(d == ND - 1))
                                     .then_inc(self.SM["pe"], 1)) if f else
                                    e.matmul(p[:, 1024 + q * 512:1024 + (q + 1) * 512],
                                             self.WB2[:, w, d, :],
                                             self.XNT[:, d, t * 1024 + q * 512:t * 1024 + (q + 1) * 512],
                                             start=(d == 0), stop=(d == ND - 1)))
                    vpe = self.bump("pe")
                    wuse1[wz] = vpe
                    self.wait("scalar", "pe", vpe)
                    self.wait("scalar", "dv", s1_dv)
                    self.em("scalar", lambda e, p=ps, ee=e_, h=hc: e.activation(
                        self.S1[:, :], p[:, 0:1024], AF.Silu,
                        bias=self.B1B[:, ee * NHT + h:ee * NHT + h + 1]))
                    vs = self.scdrain()
                    self.wait("vector", "sc", vs)
                    self.em("vector", lambda e, p=ps, ee=e_, h=hc, t=th:
                            e.scalar_tensor_tensor(
                                self.HT[:, h, t * 1024:(t + 1) * 1024], p[:, 1024:2048],
                                self.B2B[:, ee * NHT + h:ee * NHT + h + 1], self.S1[:, :],
                                ALU.add, ALU.mult))
                    self.em("vector", lambda e, h=hc, t=th: e.tensor_mul(
                        self.HT[:, h, t * 1024:(t + 1) * 1024],
                        self.HT[:, h, t * 1024:(t + 1) * 1024],
                        self.CMB[:, t * 1024:(t + 1) * 1024]).then_inc(self.SM["dv"], 1))
                    vh = self.bump("dv")
                    vh_last = vh
                    s1_dv = vh
                    rel[z] = ("dv", vh)
            self.wait("tensor", "qg", b3v)
            self.wait("tensor", "dv", vh_last)
            for oc in range(ND):
                wz3 = oc % 2
                if oc >= 2 or e_ > 0:
                    self.wait("gpsimd", "pe", wuse3[wz3])
                v3 = self.dma("qg", self.WB3[:, wz3, :, :],
                              A(self.sc_slab["w3"], (e_ * ND + oc) * 128 * NHT * 128,
                                [[NHT * 128, 128], [128, NHT], [1, 128]]))
                z, ps = slot()
                self.wait("tensor", "qg", v3)
                for ht in range(NHT):
                    for qn in range(4):
                        self.em("tensor", lambda e, w=wz3, hh=ht, q=qn, p=ps:
                                e.matmul(p[:, q * 512:(q + 1) * 512],
                                         self.WB3[:, w, hh, :],
                                         self.HT[:, hh, q * 512:(q + 1) * 512],
                                         start=(hh == 0), stop=False))
                for qn in range(4):
                    self.em("tensor", lambda e, q=qn, p=ps, o=oc, f=(qn == 3): (e.matmul(
                        p[:, q * 512:(q + 1) * 512], self.B3ROW[0:1, 128 * o:128 * o + 128],
                        self.CMB[0:1, q * 512:(q + 1) * 512], start=False, stop=True)
                        .then_inc(self.SM["pe"], 1)) if f else e.matmul(
                        p[:, q * 512:(q + 1) * 512], self.B3ROW[0:1, 128 * o:128 * o + 128],
                        self.CMB[0:1, q * 512:(q + 1) * 512], start=False, stop=True))
                vpe = self.bump("pe")
                wuse3[wz3] = vpe
                prev_p2_pe = vpe
                self.wait("scalar", "pe", vpe)
                if self._ybuf_qw:
                    self.wait("scalar", "qw", self._ybuf_qw)
                for hf in range(2):
                    self.em("scalar", lambda e, p=ps, h=hf: e.activation(
                        self.YBUF[:, h * 1024:(h + 1) * 1024], p[:, h * 1024:(h + 1) * 1024],
                        AF.Identity))
                vy = self.scdrain()
                rel[z] = ("sc", vy)
                self.wait("sync", "sc", vy)
                self.wait("sync", "dv", ytok_dv)
                tout = A(self.YTOK, 0, [[NT * 128, 128], [128, NT], [1, 128]])
                tin = A(self.YBUF, 0, [[S, 128], [128, NT], [1, 128]])
                self._ybuf_qw = self.dma("qw", tout, tin, transpose=True)
                self.wait("vector", "qw", self._ybuf_qw)
                self.em("vector", lambda e, o=oc: e.tensor_add(
                    A(self.XR, 128 * o, [[NT * D, 128], [D, NT], [1, 128]]),
                    A(self.XR, 128 * o, [[NT * D, 128], [D, NT], [1, 128]]),
                    self.YTOK[:, :, :]).then_inc(self.SM["dv"], 1))
                ytok_dv = self.bump("dv")
        self._arena_free = ("dv", self.cur("dv"))
        return ("dv", self.cur("dv"))

    def ffn(self, b, dep=None):
        sub = getattr(self, "sub", "full")
        r1 = self.norm_pass(self.XR, "ln_tr", dep=dep)
        if sub == "fln":
            return r1
        if dep is not None:
            self.wait("gpsimd", dep[0], dep[1])
        sp = self.spill_x()
        r2 = self.norm_pass(self.TR, "rms1", dep=r1)
        if sub == "frms1":
            return r2
        self.wait("sync", "qo", sp)
        ra = self.attn("inner", r2, sub={"finq": "qkv", "finr": "rope"}.get(sub, "full"))
        if sub in ("fattn", "finq", "finr"):
            return ra
        r3 = self.norm_pass(self.TR, "rms2", dep=ra)
        self.wait("gpsimd", ra[0], ra[1])
        rl = self.reload_x()
        self.wait("vector", "qg", rl)
        self.wait("vector", ra[0], ra[1])
        for tt in range(NT):
            self.em("vector", lambda e, t=tt: e.tensor_add(
                self.XR[:, t, :], self.XR[:, t, :], self.TR[:, t, :])
                .then_inc(self.SM["dv"], 1))
            self.bump("dv")
        xt = ("dv", self.cur("dv"))
        if sub == "fxt":
            return xt
        return self.moe(r3, xt)

    def module(self, b):
        lx = self.load_x(b)
        dep = ("qg", lx)
        if self.upto >= 1:
            r = self.norm_pass(self.XR, "ln_xnt", dep=dep)
            s_ = getattr(self, "sub", "full")
            if s_ == "norm":
                dep = r
            else:
                dep = self.attn("a", r, sub=s_ if s_ in ("v", "vev", "vmm", "qkv", "sloop") else "full")
        if self.upto >= 2:
            dep = self.ffn(b, dep)
        if self.upto >= 3:
            r = self.norm_pass(self.XR, "ln_xnt", dep=dep)
            dep = self.attn("b", r)
        if self.upto >= 4:
            dep = self.ffn(b, dep)
        dmp = getattr(self, "dump", None)
        if dmp is not None and b == 0:
            self.wait("gpsimd", dep[0], dep[1])
            self.barrier()
            doff = 0
            for nm in dmp.split(","):
                buf = getattr(self, nm)
                n = 1
                for s_ in buf.shape:
                    n *= s_
                if buf.shape[0] == 1:
                    self.dma("qo", A(self.xout, S * D + doff, [[n, 1], [1, n]]),
                             A(buf, 0, [[n, 1], [1, n]]))
                else:
                    free = min(n // 128, (S * D - doff) // 128)
                    self.dma("qo", A(self.xout, S * D + doff, [[free, 128], [1, free]]),
                             A(buf, 0, [[free, 128], [1, free]]))
                doff += n
        self.wait("gpsimd", dep[0], dep[1])
        self.barrier()
        return self.store_x(b)

    def build_all(self):
        self._ybuf_qw = 0
        cvals = [1e-5, 1e-6]
        for i, cvv in enumerate(cvals):
            self.em("vector", lambda e, ii=i, vv2=cvv: e.memset(self.CPOOL[:, ii:ii + 1], vv2))
            self.nc.const_aps.register(F32, cvv, self.CPOOL[:, i:i + 1])
        self.em("vector", lambda e: e.nop().then_inc(self.SM["dv"], 1))
        cb = self.bump("dv")
        for eng in ("scalar", "tensor", "gpsimd", "sync"):
            self.wait(eng, "dv", cb)
        cl = self.prep_const_loads()
        if self.upto >= 1:
            self.prep_weights()
        nb = 1 if getattr(self, "dump", None) else 2
        for b in range(nb):
            self.module(b)
        self.wait("gpsimd", "qo", self.cur("qo"))
        return self.finalize()


_UPTO = 4


def kernel(**inputs):
    cons = host_constants()
    i = inputs
    qkvb = np.asarray(i["L0_ffn_qkv_b"])
    common = {
        "aW": np.asarray(i["L0_attn_a_W_attn"]), "aP": np.asarray(i["L0_attn_a_W_proj"]),
        "bWq": np.asarray(i["L0_attn_b_Wq"]), "bWk": np.asarray(i["L0_attn_b_Wk"]),
        "bWv": np.asarray(i["L0_attn_b_Wv"]), "bWo": np.asarray(i["L0_attn_b_Wo"]),
        "fqkv": np.asarray(i["L0_ffn_qkv_w"]), "fout": np.asarray(i["L0_ffn_out_w"]),
        "fgate": np.asarray(i["L0_ffn_gate_w"]),
        "fw1": np.asarray(i["L0_ffn_w1"]), "fw2": np.asarray(i["L0_ffn_w2"]),
        "fw3": np.asarray(i["L0_ffn_w3"]),
        "bq_pp": _pp(np.asarray(i["L0_attn_b_bq"])), "bk_pp": _pp(np.asarray(i["L0_attn_b_bk"])),
        "bv_pp": _pp(np.asarray(i["L0_attn_b_bv"])),
        "fbq_pp": _pp(qkvb[0:D]), "fbk_pp": _pp(qkvb[D:2 * D]), "fbv_pp": _pp(qkvb[2 * D:3 * D]),
        "b1_pp": np.ascontiguousarray(
            np.asarray(i["L0_ffn_b1"]).reshape(E, NHT, 128).transpose(2, 0, 1).reshape(128, E * NHT)),
        "b2_pp": np.ascontiguousarray(
            np.asarray(i["L0_ffn_b2"]).reshape(E, NHT, 128).transpose(2, 0, 1).reshape(128, E * NHT)),
        "bo_row": np.ascontiguousarray(np.asarray(i["L0_attn_b_bo"])[None, :]),
        "fob_row": np.ascontiguousarray(np.asarray(i["L0_ffn_out_b"])[None, :]),
        "b3_r": np.asarray(i["L0_ffn_b3"]),
        "g1_rep": np.ascontiguousarray(
            np.broadcast_to(np.asarray(i["L0_ffn_norm1_w"]), (128, D))),
        "g2_rep": np.ascontiguousarray(
            np.broadcast_to(np.asarray(i["L0_ffn_norm2_w"]), (128, D))),
    }
    for k, v in cons.items():
        common["c_" + k] = v
    x = np.asarray(i["L0_attn_a_x"])
    bld = Build(upto=_UPTO)
    nc = bld.build_all()
    in_maps = []
    for core in range(8):
        m = dict(common)
        m["xin"] = np.ascontiguousarray(x[2 * core:2 * core + 2])
        in_maps.append(m)
    res = run_bass_kernel_spmd(nc, in_maps, core_ids=list(range(8)))
    out = np.concatenate([r["xout"] for r in res.results], axis=0)
    return out.astype(np.float32)
